# version 05
# baseline (speedup 1.0000x reference)
import numpy as np
import ml_dtypes
from contextlib import ExitStack

import concourse.bass as bass
import concourse.mybir as mybir
from concourse.bass_utils import run_bass_kernel_spmd

F32 = mybir.dt.float32
BF16 = mybir.dt.bfloat16
I32 = mybir.dt.int32
AF = mybir.ActivationFunctionType
ALU = mybir.AluOpType
AX = mybir.AxisListType.X

B, T, D, DC, DR, NH, HD = 8, 2048, 1024, 128, 32, 16, 64
NT = T // 128
NCH = D // 128
SCALE = float((HD + DR) ** -0.5)
SHIFT = 12.0
EPS_LN = 1e-5
EPS_FFN = 1e-8

NPBF = ml_dtypes.bfloat16


class Builder:
    ENGINES = ("sync", "gpsimd", "tensor", "vector", "scalar")

    def __init__(self, debug=False, stop=None):
        self.debug = debug
        self.stop = stop
        self.nc = bass.Bass("TRN2", target_bir_lowering=False)
        self.ops = {e: [] for e in self.ENGINES}
        self.semv = {}
        self.sems = {}
        self.ctx = ExitStack()
        self.lastw = {}
        self.build()

    def sem(self, name):
        self.sems[name] = self.ctx.enter_context(self.nc.semaphore(name))
        self.semv[name] = 0

    def w(self, name):
        return (name, self.semv[name])

    def emit(self, eng, fn, waits=(), inc=None, incn=1):
        waits = [(s, v) for (s, v) in waits if v > 0]
        for s, v in waits:
            assert s == "MODEV" or v <= self.semv[s], f"wait on {s}={v} > planned {self.semv[s]}"
        if inc is not None:
            self.semv[inc] += incn
        sems, lastw = self.sems, self.lastw

        def run(e, _fn=fn, _w=waits, _i=inc, _e=eng):
            for s, v in _w:
                k = (_e, s)
                if lastw.get(k, -1) < v:
                    e.wait_ge(sems[s], v)
                    lastw[k] = v
            r = _fn(e)
            if _i is not None:
                r.then_inc(sems[_i], incn)

        self.ops[eng].append(run)

    def dma(self, eng, out, in_, waits=(), inc=None, transpose=False):
        if transpose:
            self.emit(eng, lambda e, o=out, i=in_: e.dma_start_transpose(o, i),
                      waits=waits, inc=inc, incn=16)
        else:
            self.emit(eng, lambda e, o=out, i=in_: e.dma_start(o, i),
                      waits=waits, inc=inc, incn=16)

    def mm(self, out, lhsT, rhs, start=True, stop=True, waits=(), tp=None):
        self.emit("tensor",
                  lambda e, o=out, l=lhsT, r=rhs, s1=start, s2=stop, t=tp:
                  e.matmul(o, l, r, start=s1, stop=s2, tile_position=t),
                  waits=waits, inc="T")
        return self.semv["T"]

    def build(self):
        nc = self.nc
        E = self.ctx.enter_context
        ap = lambda t: t.ap()

        self.h_x = ap(nc.dram_tensor("x", [T, D], F32, kind="ExternalInput"))
        self.h_wkv = ap(nc.dram_tensor("w_kv_down", [D, DC], F32, kind="ExternalInput"))
        self.h_wku = ap(nc.dram_tensor("w_k_up", [DC, D], F32, kind="ExternalInput"))
        self.h_wvu = ap(nc.dram_tensor("w_v_up", [DC, D], F32, kind="ExternalInput"))
        self.h_wq = ap(nc.dram_tensor("w_q", [D, D], F32, kind="ExternalInput"))
        self.h_wqr = ap(nc.dram_tensor("w_q_rope", [D, DR], F32, kind="ExternalInput"))
        self.h_wkr = ap(nc.dram_tensor("w_k_rope", [DC, DR], F32, kind="ExternalInput"))
        self.h_wo = ap(nc.dram_tensor("w_o", [D, D], F32, kind="ExternalInput"))
        self.h_wf = ap(nc.dram_tensor("w_ffn", [D, D], F32, kind="ExternalInput"))
        self.h_wmod = ap(nc.dram_tensor("w_mod", [D, 6 * D], F32, kind="ExternalInput"))
        self.h_cT = ap(nc.dram_tensor("cT", [128, 8], F32, kind="ExternalInput"))
        self.h_cossin = ap(nc.dram_tensor("cossin", [32, T], BF16, kind="ExternalInput"))
        self.h_trimask = ap(nc.dram_tensor("trimask", [128, 128], BF16, kind="ExternalInput"))
        self.h_pairsel = ap(nc.dram_tensor("pairsel", [16, 1024], BF16, kind="ExternalInput"))
        self.h_out = ap(nc.dram_tensor("out", [T, D], F32, kind="ExternalOutput"))
        if self.debug:
            self.h_dbg = {}
            for name, shape, dt in [("mod", [1, 6 * D], BF16), ("bigA", [128, 16384], BF16),
                                    ("bigB", [128, 16384], BF16), ("ckvT", [128, T], BF16),
                                    ("aux1", [128, T], BF16), ("wmodb", [128, 2048], BF16),
                                    ("stats", [128, 320], F32), ("smalls", [128, 32], F32)]:
                self.h_dbg[name] = ap(nc.dram_tensor("dbg_" + name, shape, dt,
                                                     kind="ExternalOutput"))

        sb = lambda name, shape, dt: E(nc.sbuf_tensor(name, shape, dt))
        self.x_sb = sb("x_sb", [128, NT * D], F32)
        self.bigA = sb("bigA", [128, 16384], BF16)
        self.bigB = sb("bigB", [128, 16384], BF16)
        self.mod_row = sb("mod_row", [1, 6 * D], BF16)
        self.ckvT = sb("ckvT", [128, T], BF16)
        self.aux1 = sb("aux1", [128, T], BF16)
        self.cossin = sb("cossin_sb", [128, 2 * T], BF16)
        self.rep = sb("rep", [128, 2 * D], BF16)
        self.wmodb = sb("wmodb", [128, 2048], BF16)
        self.wqwo = sb("wqwo", [128, NCH * D], BF16)
        self.wkv = sb("wkv", [128, NCH * DC], BF16)
        self.wqr = sb("wqr", [128, NCH * DR], BF16)
        self.wku = sb("wku", [128, D], BF16)
        self.wvu = sb("wvu", [128, D], BF16)
        self.wkr = sb("wkr", [128, DR], BF16)
        self.tq = sb("tq", [128, D], F32)
        self.tm = sb("tm", [128, D], F32)
        self.h1b = sb("h1b", [128, 2 * D], BF16)
        self.zi = sb("zi", [128, D], I32)
        self.cpool = sb("cpool", [128, 12], F32)
        self.cT = sb("cT_sb", [128, 8], F32)
        self.cTb = sb("cTb", [128, 8], BF16)
        self.trimask = sb("trimask_sb", [128, 128], BF16)
        self.pairsel = sb("pairsel_sb", [16, 1024], BF16)
        self.ones1 = sb("ones1", [1, 128], BF16)
        self.ones1f = sb("ones1f", [1, 128], F32)
        self.ones128f = sb("ones128f", [128, 1], F32)
        self.stats = sb("stats", [128, 320], F32)
        self.smalls = sb("smalls", [128, 32], F32)
        self.wsrow = sb("wsrow", [1, 8], F32)

        self.consts = [-SHIFT, SCALE, 1.0, 1.0 / D, EPS_LN, 1.0 / (D * D), EPS_FFN,
                       -1.0 / D, 1.0 / 127.0, 127.0]
        for i, v in enumerate(self.consts):
            nc.const_aps.register(F32, float(v), self.cpool[:, i:i + 1])

        self.psA = E(nc.psum_tensor("psA", [128, 1024], F32))
        self.psB = E(nc.psum_tensor("psB", [128, 1024], F32))
        self.psM = E(nc.psum_tensor("psM", [128, 2048], F32))

        for s in ("T", "V", "S", "W", "WO", "CONST", "MODA", "MODB",
                  "SX0", "SX1", "SX2", "SX3", "TRH0", "TRH1", "TRZ0", "TRZ1",
                  "TRW", "WFA", "WFB", "OUT", "QPK", "OAT", "RCP0", "RCP1", "RP", "MODEV"):
            self.sem(s)

        order = ["p0", "p1", "p2", "p3", "p4", "p5"]
        fns = [self.phase0, self.phase1, self.phase2, self.phase3, self.phase4, self.phase5]
        for nm, fn in zip(order, fns):
            fn()
            if self.stop == nm:
                break
        self.finish()

    def phase0(self):
        def init(e):
            for i, v in enumerate(self.consts):
                e.memset(self.cpool[:, i:i + 1], float(v))
            e.memset(self.ones1[:], 1.0)
            e.memset(self.ones1f[:], 1.0)
            return e.memset(self.ones128f[:], 1.0)
        self.emit("vector", init, inc="V")
        self.v_init = self.semv["V"]

        self.dma("sync", self.cT[:], self.h_cT[:, :], inc="CONST")
        for q in range(4):
            self.dma("sync", self.x_sb.ap().rearrange("p (i d) -> p i d", d=D)[:, 4 * q:4 * q + 4, :],
                     self.h_x.rearrange("(i p) d -> p i d", p=128)[:, 4 * q:4 * q + 4, :],
                     inc=f"SX{q}")
        self.dma("sync", self.cossin[64:80, 0:T], self.h_cossin[0:16, :], inc="CONST")
        self.dma("sync", self.cossin[64:80, T:2 * T], self.h_cossin[16:32, :], inc="CONST")
        self.dma("sync", self.trimask[:], self.h_trimask[:, :], inc="CONST")
        self.dma("sync", self.pairsel[:], self.h_pairsel[:, :], inc="CONST")
        self.const_total = self.semv["CONST"]

        def emit_weight_loads():
            self.dma("gpsimd", self.wqwo.ap().rearrange("p (j o) -> p j o", o=D),
                     self.h_wq.rearrange("(j p) o -> p j o", p=128), inc="W")
            self.dma("gpsimd", self.wkv.ap().rearrange("p (j o) -> p j o", o=DC),
                     self.h_wkv.rearrange("(j p) o -> p j o", p=128), inc="W")
            self.dma("gpsimd", self.wqr.ap().rearrange("p (j o) -> p j o", o=DR),
                     self.h_wqr.rearrange("(j p) o -> p j o", p=128), inc="W")
            self.dma("gpsimd", self.wku[:], self.h_wku[:, :], inc="W")
            self.dma("gpsimd", self.wvu[:], self.h_wvu[:, :], inc="W")
            self.dma("gpsimd", self.wkr[:], self.h_wkr[:, :], inc="W")
            self.w_all = self.w("W")

        self.emit("scalar", lambda e: e.activation(self.cT[:], self.cT[:], AF.Silu),
                  waits=[("CONST", self.const_total)], inc="S")
        self.emit("scalar", lambda e: e.copy(self.cTb[:], self.cT[:]), inc="S")
        s_ctb = self.semv["S"]

        slots = [self.wmodb[:, 0:1024], self.wmodb[:, 1024:2048]]
        names = ["MODA", "MODB"]
        self.pending_mod_evacs = []
        mm_done = {}
        blk = 0
        for g in range(6):
            half = (g % 2) * 1024
            for j in range(8):
                slot = blk % 2
                dwaits = [("T", mm_done[blk - 2])] if blk >= 2 else []
                self.dma("gpsimd", slots[slot],
                         self.h_wmod[128 * j:128 * (j + 1), 1024 * g:1024 * (g + 1)],
                         waits=dwaits, inc=names[slot])
                mwaits = [(names[slot], 16 * (blk // 2 + 1)), ("S", s_ctb)]
                if g >= 2 and j == 0:
                    mwaits.append(("MODEV", g - 1))
                for p in range(2):
                    self.mm(self.psM[0:1, half + 512 * p: half + 512 * (p + 1)],
                            self.cTb[:, j:j + 1], slots[slot][:, 512 * p:512 * (p + 1)],
                            start=(j == 0), stop=(j == 7),
                            waits=mwaits if p == 0 else [])
                mm_done[blk] = self.semv["T"]
                blk += 1
                if blk == 8:
                    emit_weight_loads()
            tg = self.semv["T"]
            if g < 2:
                self._emit_one_mod_evac(g, half, tg)
            else:
                self.pending_mod_evacs.append((g, half, tg))
        self.mod_stream_done = [("MODA", 16 * 24), ("MODB", 16 * 24)]

    def _emit_one_mod_evac(self, g, half, tg):
        dst = self.mod_row[0:1, 1024 * g:1024 * (g + 1)]
        if g in (1, 4):
            self.emit("scalar", lambda e, d=dst, hh=half: e.activation(
                d, self.psM[0:1, hh:hh + 1024], AF.Copy, bias=1.0),
                waits=[("T", tg)], inc="MODEV")
        else:
            self.emit("scalar", lambda e, d=dst, hh=half: e.copy(
                d, self.psM[0:1, hh:hh + 1024]),
                waits=[("T", tg)], inc="MODEV")

    def emit_mod_evacs(self, upto):
        while self.pending_mod_evacs and self.pending_mod_evacs[0][0] < upto:
            g, half, tg = self.pending_mod_evacs.pop(0)
            self._emit_one_mod_evac(g, half, tg)

    def replicate_mod(self, vec_idx, slot, clamp=False):
        ps = self.psA
        mmw = [("MODEV", vec_idx + 1), self.w("V"), self.w("S")]
        for p in range(2):
            self.mm(ps[:, 512 * p:512 * (p + 1)], self.ones1[:],
                    self.mod_row[0:1, 1024 * vec_idx + 512 * p:1024 * vec_idx + 512 * (p + 1)],
                    waits=mmw if p == 0 else [])
        t_done = self.semv["T"]
        dst = self.rep[:, 1024 * slot:1024 * (slot + 1)]
        if not clamp:
            self.emit("vector", lambda e: e.tensor_copy(dst, ps[:, 0:1024]),
                      waits=[("T", t_done)], inc="V")
        else:
            tq, tm = self.tq, self.tm
            self.emit("vector", lambda e: e.tensor_scalar(tq[:], ps[:, 0:1024], 0.0, None, ALU.is_ge),
                      waits=[("T", t_done)], inc="V")
            self.emit("vector", lambda e: e.tensor_scalar(tm[:], ps[:, 0:1024], 0.25, None, ALU.max), inc="V")
            self.emit("vector", lambda e: e.tensor_scalar(dst, ps[:, 0:1024], -0.25, None, ALU.min), inc="V")
            self.emit("vector", lambda e: e.tensor_sub(tm[:], tm[:], dst), inc="V")
            self.emit("vector", lambda e: e.tensor_mul(tm[:], tm[:], tq[:]), inc="V")
            self.emit("vector", lambda e: e.tensor_add(dst, dst, tm[:]), inc="V")
        return self.semv["V"], t_done

    def phase1(self):
        sqv = lambda i: self.stats[:, 256 + 4 * i:256 + 4 * i + 1]
        negm = lambda i: self.stats[:, 4 * i:4 * i + 1]
        varc = lambda i: self.stats[:, 64 + 4 * i:64 + 4 * i + 1]
        rstd = lambda i: self.stats[:, 128 + 4 * i:128 + 4 * i + 1]
        xt = lambda i: self.x_sb[:, D * i:D * (i + 1)]

        self.sev_rstd = {}
        for i in range(NT):
            self.emit("vector", lambda e, ii=i: e.reduce_sum(varc(ii), xt(ii), axis=AX),
                      waits=[(f"SX{i // 4}", 16), ("V", self.v_init)], inc="V")
            vm = self.semv["V"]
            self.emit("scalar", lambda e, ii=i: e.activation(negm(ii), varc(ii), AF.Copy,
                                                             scale=-1.0 / D),
                      waits=[("V", vm)], inc="S")
            sn = self.semv["S"]
            self.emit("vector", lambda e, ii=i: e.tensor_scalar(self.tq[:], xt(ii), negm(ii), None,
                                                                ALU.add),
                      waits=[("S", sn)], inc="V")
            self.emit("vector", lambda e: e.tensor_mul(self.tm[:], self.tq[:], self.tq[:]), inc="V")
            self.emit("vector", lambda e, ii=i: e.reduce_sum(varc(ii), self.tm[:], axis=AX), inc="V")
            vv = self.semv["V"]
            self.emit("scalar", lambda e, ii=i: e.activation(sqv(ii), varc(ii), AF.Sqrt,
                                                             bias=EPS_LN, scale=1.0 / D),
                      waits=[("V", vv)], inc="S")
            sv = self.semv["S"]
            self.emit("vector", lambda e, ii=i: e.reciprocal(rstd(ii), sqv(ii)),
                      waits=[("S", sv)], inc="V")
            self.sev_rstd[i] = self.semv["V"]

        self.emit_mod_evacs(6)
        v_sa, _ = self.replicate_mod(0, 0)
        v_ca, self.t_carep = self.replicate_mod(1, 1)
        sa_rep = self.rep[:, 0:1024]
        ca_rep = self.rep[:, 1024:2048]

        h1T_dst = self.bigA.ap().rearrange("p (j t) -> p j t", t=T)
        trh = [0, 0]
        for i in range(NT):
            slot = i % 2
            h1tile = self.h1b[:, D * slot:D * (slot + 1)]
            waits = [("V", max(v_sa, v_ca, self.sev_rstd[i]))]
            if i >= 2:
                waits.append((f"TRH{slot}", 16 * trh[slot]))
            self.emit("vector",
                      lambda e, ii=i: e.scalar_tensor_tensor(self.tm[:], xt(ii), negm(ii), ca_rep,
                                                             ALU.add, ALU.mult),
                      waits=waits, inc="V")
            self.emit("vector",
                      lambda e, ii=i, h1=h1tile: e.scalar_tensor_tensor(h1, self.tm[:], rstd(ii), sa_rep,
                                                                        ALU.mult, ALU.add),
                      inc="V")
            vap = self.semv["V"]
            self.dma("scalar", h1T_dst[:, :, 128 * i:128 * (i + 1)], h1tile,
                     waits=[("V", vap)], inc=f"TRH{slot}", transpose=True)
            trh[slot] += 1
        self.p1_done_w = [("TRH0", 16 * trh[0]), ("TRH1", 16 * trh[1])]

    def phase2(self):
        h1T = lambda j: self.bigA[:, T * j:T * (j + 1)]

        for p in range(4):
            for j in range(NCH):
                w0 = []
                if p == 0 and j == 0:
                    w0 = self.p1_done_w + [self.w_all, ("MODEV", 6)]
                self.mm(self.psM[:, 512 * p:512 * (p + 1)],
                        self.wkv[:, DC * j:DC * (j + 1)],
                        h1T(j)[:, 512 * p:512 * (p + 1)],
                        start=(j == 0), stop=(j == NCH - 1), waits=w0)
        t_ckv = self.semv["T"]
        self.emit("scalar", lambda e: e.copy(self.ckvT[:], self.psM[:, 0:2048]),
                  waits=[("T", t_ckv)], inc="S")
        s_ckv = self.semv["S"]

        def rope(dst, stat_x1, stat_x2, nacc, mov_of, extra_first=(), extra_v=()):
            cosv = self.cossin[64:80, :]
            first = list(extra_first)
            v_last = 0
            for q in range(4):
                lo = 512 * q
                ps = self.psA if q % 2 == 0 else self.psB
                for j in range(nacc):
                    self.mm(ps[64:80, 0:512], stat_x1(j), mov_of(j, lo, 512),
                            start=(j == 0), stop=(j == nacc - 1),
                            waits=(first + ([("V", v_last)] if q >= 2 else [])) if j == 0 else [],
                            tp=(0, 64))
                    first = []
                    self.mm(ps[64:80, 512:1024], stat_x2(j), mov_of(j, lo, 512),
                            start=(j == 0), stop=(j == nacc - 1), tp=(0, 64))
                t_mm = self.semv["T"]
                x1 = ps[64:80, 0:512]
                x2 = ps[64:80, 512:1024]
                cosp = cosv[:, lo:lo + 512]
                sinp = cosv[:, T + lo:T + lo + 512]
                th = self.tq[64:80, 0:512]
                uh = self.tq[64:80, 512:1024]
                self.emit("vector", lambda e, a=x1, b=cosp, c=th: e.tensor_mul(c, a, b),
                          waits=[("T", t_mm), ("CONST", self.const_total)] + list(extra_v), inc="V")
                extra_v = ()
                self.emit("vector", lambda e, a=x2, b=sinp, c=uh: e.tensor_mul(c, a, b), inc="V")
                self.emit("vector", lambda e, a=th, b=uh, c=dst[64:80, lo:lo + 512]:
                          e.tensor_sub(c, a, b), inc="V")
                self.emit("vector", lambda e, a=x2, b=cosp, c=th: e.tensor_mul(c, a, b), inc="V")
                self.emit("vector", lambda e, a=x1, b=sinp, c=uh: e.tensor_mul(c, a, b), inc="V")
                self.emit("vector", lambda e, a=th, b=uh, c=self.h1b[64:80, lo:lo + 512]:
                          e.tensor_add(c, a, b), inc="V")
                v_last = self.semv["V"]
            self.dma("gpsimd", dst[96:112, 0:T], self.h1b[64:80, 0:T],
                     waits=[("V", v_last)] + self.p1_done_w, inc="RP")
            return v_last

        rope(self.wmodb,
             lambda j: self.wkr[:, 0:16], lambda j: self.wkr[:, 16:32], 1,
             lambda j, lo, w: self.ckvT[:, lo:lo + w],
             extra_first=[("S", s_ckv), self.w("V")] + self.mod_stream_done)

        s_prev_evac = s_ckv
        for m in range(NCH):
            for j in range(NCH):
                for p in range(4):
                    w0 = []
                    if j == 0 and p == 0:
                        w0 = [("S", s_prev_evac)]
                    self.mm(self.psM[:, 512 * p:512 * (p + 1)],
                            self.wqwo[:, D * j + 128 * m:D * j + 128 * (m + 1)],
                            h1T(j)[:, 512 * p:512 * (p + 1)],
                            start=(j == 0), stop=(j == NCH - 1), waits=w0)
            t_q = self.semv["T"]
            self.emit("scalar", lambda e, mm_=m: e.copy(self.bigB[:, T * mm_:T * (mm_ + 1)],
                                                        self.psM[:, 0:2048]),
                      waits=[("T", t_q)], inc="S")
            s_prev_evac = self.semv["S"]
        self.sev_qT = self.semv["S"]

        v_qr = rope(self.aux1,
                    lambda j: self.wqr[:, DR * j:DR * j + 16],
                    lambda j: self.wqr[:, DR * j + 16:DR * j + 32], NCH,
                    lambda j, lo, w: h1T(j)[:, lo:lo + w], extra_v=[("RP", 16)])
        self.sev_rope = v_qr
        self.t_h1T_done = self.semv["T"]
        self.rp_total = self.semv["RP"]

    def phase3(self):
        KH = [self.bigA[:, 0:T], self.bigA[:, T:2 * T]]
        QH = [self.bigA[:, 2 * T:3 * T], self.bigA[:, 3 * T:4 * T]]
        VH = [self.bigA[:, 8192:8192 + 1040], self.bigA[:, 9232:9232 + 1040]]
        STG = [self.bigA[:, 10272 + 1024 * s:10272 + 1024 * (s + 1)] for s in range(4)]
        chunk = lambda c: self.bigB[:, T * c:T * (c + 1)]

        w_hd = [("T", self.t_h1T_done), ("RP", self.rp_total)]
        for pi in range(2):
            self.emit("vector", lambda e, b=KH[pi]: e.memset(b[64:128, :], 0.0),
                      waits=w_hd, inc="V")
            w_hd = []
            self.emit("vector", lambda e, b=QH[pi]: e.memset(b[64:128, :], 0.0), inc="V")
            self.emit("vector", lambda e, b=KH[pi]: e.tensor_copy(b[64:80, :], self.wmodb[64:80, :]),
                      inc="V")
            self.emit("vector", lambda e, b=KH[pi]: e.tensor_copy(b[96:112, :], self.wmodb[96:112, :]),
                      inc="V")
            self.emit("vector", lambda e, b=QH[pi]: e.tensor_copy(b[64:80, :], self.aux1[64:80, :]),
                      inc="V")
            self.emit("vector", lambda e, b=QH[pi]: e.tensor_copy(b[96:112, :], self.aux1[96:112, :]),
                      inc="V")
        self.emit("vector", lambda e: e.memset(VH[0], 1.0), inc="V")
        self.emit("vector", lambda e: e.memset(VH[1], 1.0), inc="V")
        v_ones = self.semv["V"]

        exp_done_ps = {}
        self.recip_rows = 0
        qpk = [0]

        scores_done_prev = 0
        av_done_prev = 0
        v_prep = v_ones
        v_half1_evac_prev = 0

        def emit_prep(h, sc_prev, av_prev, psab_free, psm_ok):
            pi = h % 2
            for p in range(2):
                w0 = list(psm_ok) if p == 0 else []
                if p == 0 and psab_free:
                    w0.append(("S", psab_free))
                self.mm(self.psM[0:64, 512 * p:512 * (p + 1)],
                        self.wku[:, HD * h:HD * (h + 1)],
                        self.ckvT[:, 512 * p:512 * (p + 1)], waits=w0)
            t1 = self.semv["T"]
            self.emit("vector", lambda e, pi_=pi: e.tensor_copy(KH[pi_][0:64, 0:1024],
                                                                self.psM[0:64, 0:1024]),
                      waits=[("T", max(t1, sc_prev))], inc="V")
            v1 = self.semv["V"]
            for p in range(2):
                w0 = [("V", v1)] if p == 0 else []
                self.mm(self.psM[0:64, 512 * p:512 * (p + 1)],
                        self.wku[:, HD * h:HD * (h + 1)],
                        self.ckvT[:, 1024 + 512 * p:1024 + 512 * (p + 1)], waits=w0)
            t2 = self.semv["T"]
            self.emit("vector", lambda e, pi_=pi: e.tensor_copy(KH[pi_][0:64, 1024:2048],
                                                                self.psM[0:64, 0:1024]),
                      waits=[("T", t2)], inc="V")
            v2 = self.semv["V"]
            for j in range(NT):
                w0 = [("V", v2)] if j == 0 else []
                self.mm(self.psM[:, 64 * j:64 * (j + 1)],
                        self.ckvT[:, 128 * j:128 * (j + 1)],
                        self.wvu[:, HD * h:HD * (h + 1)], waits=w0)
            t_vup = self.semv["T"]
            self.emit("vector",
                      lambda e, pi_=pi: e.tensor_copy(
                          VH[pi_].rearrange("p (j g) -> p j g", g=65)[:, :, 0:64],
                          self.psM.ap().rearrange("p (j g) -> p j g", g=64)[:, 0:16, :]),
                      waits=[("T", max(t_vup, av_prev)), ("V", v_ones)], inc="V")
            if h % 2 == 0:
                self.emit("vector", lambda e, hh=h, pi_=pi: e.tensor_copy(
                    QH[pi_][0:64, :], chunk(hh // 2)[0:64, :]),
                    waits=[("S", self.sev_qT), ("T", sc_prev)], inc="V")
            else:
                self.dma("sync", QH[pi][0:64, :], chunk(h // 2)[64:128, :],
                         waits=[("T", sc_prev), ("S", self.sev_qT)], inc="QPK")
                qpk[0] += 1
            return self.semv["V"], qpk[0]

        v_prep, q_prep = emit_prep(0, 0, 0, 0, [self.w("S"), self.w("V"), ("V", self.sev_rope)])

        for h in range(NH):
            pi = h % 2
            blkctr = 0
            binfo_of_j = {}
            first_w = [("V", v_prep)]
            if pi == 1:
                first_w.append(("QPK", 16 * q_prep))
            mid_done = {}

            def do_scores(j):
                nonlocal blkctr, first_w
                q0 = 128 * j
                binfo = []
                bstart = q0
                while bstart < T:
                    wblk = min(1024, T - bstart)
                    par = blkctr % 2
                    ps = self.psA if par == 0 else self.psB
                    stg = blkctr % 4
                    w0 = first_w
                    first_w = []
                    if par in exp_done_ps:
                        w0 = w0 + [("S", exp_done_ps[par])]
                    off = 0
                    while off < wblk:
                        pw = min(512, wblk - off)
                        self.mm(ps[:, off:off + pw],
                                KH[pi][:, 128 * j:128 * (j + 1)],
                                QH[pi][:, bstart + off:bstart + off + pw], waits=w0)
                        w0 = []
                        off += 512
                    t_mm = self.semv["T"]
                    self.emit("scalar",
                              lambda e, ps_=ps, w_=wblk, s_=stg: e.activation(
                                  STG[s_][:, 0:w_], ps_[:, 0:w_], AF.Exp,
                                  bias=-SHIFT, scale=SCALE),
                              waits=[("T", t_mm)], inc="S")
                    exp_done_ps[par] = self.semv["S"]
                    s_exp = self.semv["S"]
                    v_mask = 0
                    if bstart == q0:
                        self.emit("vector",
                                  lambda e, s_=stg: e.tensor_mul(
                                      STG[s_][:, 0:128], STG[s_][:, 0:128], self.trimask[:]),
                                  waits=[("S", s_exp)], inc="V")
                        v_mask = self.semv["V"]
                    binfo.append((bstart, wblk, stg, s_exp, v_mask))
                    blkctr += 1
                    bstart += wblk
                binfo_of_j[j] = binfo

            def do_av(j, half1_guard):
                for (bstart, wblk, stg, s_exp, v_mask) in binfo_of_j.pop(j):
                    s0 = bstart
                    while s0 < bstart + wblk:
                        s1 = min((s0 // 512 + 1) * 512, bstart + wblk)
                        gw = s1 - s0
                        reg_last = min(NT - 1, (s1 - 1) // 128)
                        w0 = [("S", s_exp)]
                        if v_mask:
                            w0.append(("V", v_mask))
                        if j == 0 and half1_guard:
                            w0.append(("V", half1_guard))
                        self.mm(self.psM[0:65, s0:s1],
                                VH[pi][:, 65 * j:65 * j + 65],
                                STG[stg][:, s0 - bstart:s0 - bstart + gw],
                                start=(j == 0), stop=(j == reg_last), waits=w0)
                        s0 = s1

            def evac_attn_half(h_, half, t_guard):
                pi_ = h_ % 2
                lo = 1024 * half
                dst_rows = (64 * (h_ % 2), 64 * (h_ % 2) + 64)
                if h_ % 2 == 0:
                    self.emit("vector", lambda e, hh=h_, l=lo: e.tensor_copy(
                        chunk(hh // 2)[0:64, l:l + 1024], self.psM[0:64, l:l + 1024]),
                        waits=[("T", t_guard)], inc="V")
                else:
                    self.emit("vector", lambda e, hh=h_, l=lo: e.tensor_copy(
                        self.h1b[0:64, l:l + 1024], self.psM[0:64, l:l + 1024]),
                        waits=[("T", t_guard), ("OAT", 16 * (2 * (h_ // 2) + half))], inc="V")
                    v_st = self.semv["V"]
                    self.dma("sync", chunk(h_ // 2)[64:128, lo:lo + 1024],
                             self.h1b[0:64, lo:lo + 1024],
                             waits=[("V", v_st)], inc="OAT")
                dslot = 64 + 32 * (h_ % 2)
                self.emit("vector", lambda e, l=lo, ds=dslot: e.tensor_copy(
                    self.h1b[ds:ds + 1, l:l + 1024], self.psM[64:65, l:l + 1024]),
                    waits=[("T", t_guard), (f"RCP{h_ % 2}", 16 * (h_ // 2))], inc="V")
                return self.semv["V"]

            for j in range(NT):
                do_scores(j)
                if j >= 1:
                    do_av(j - 1, v_half1_evac_prev if True else 0)
                if j == 9:
                    t_av7 = self.semv["T"]
                    v_h0 = evac_attn_half(h, 0, t_av7)
                    if h + 1 < NH:
                        v_prep, q_prep = emit_prep(
                            h + 1, scores_done_prev, av_done_prev, 0, [("V", v_h0)])
            scores_done_cur = self.semv["T"]
            do_av(NT - 1, v_half1_evac_prev)
            t_av_all = self.semv["T"]
            v_h1 = evac_attn_half(h, 1, t_av_all)
            v_half1_evac_prev = v_h1
            dslot = 64 + 32 * (h % 2)
            self.dma("gpsimd", self.aux1[h:h + 1, :], self.h1b[dslot:dslot + 1, 0:2048],
                     waits=[("V", v_h1)], inc=f"RCP{h % 2}")
            scores_done_prev = scores_done_cur
            av_done_prev = t_av_all

        self.sev_attn_s = self.semv["S"]
        self.emit("vector", lambda e: e.reciprocal(self.tq[0:16, 0:1024], self.aux1[0:16, 0:1024]),
                  waits=[("RCP0", 16 * 8), ("RCP1", 16 * 8)], inc="V")
        self.emit("vector", lambda e: e.reciprocal(self.tm[0:16, 0:1024], self.aux1[0:16, 1024:2048]),
                  inc="V")
        self.emit("vector", lambda e: e.tensor_copy(self.aux1[0:16, 0:1024], self.tq[0:16, 0:1024]),
                  inc="V")
        self.emit("vector", lambda e: e.tensor_copy(self.aux1[0:16, 1024:2048], self.tm[0:16, 0:1024]),
                  inc="V")
        self.v_recip = self.semv["V"]

    def phase4(self):
        chunk = lambda c: self.bigB[:, T * c:T * (c + 1)]
        xt = lambda i: self.x_sb[:, D * i:D * (i + 1)]

        self.dma("gpsimd", self.wqwo.ap().rearrange("p (j o) -> p j o", o=D),
                 self.h_wo.rearrange("(j p) o -> p j o", p=128),
                 waits=[("S", self.sev_qT)], inc="WO")

        v_scaled = {-1: 0}
        for c in range(NCH):
            for p in range(4):
                w0 = []
                if p == 0:
                    w0 = [("V", max(self.v_recip, v_scaled[c - 1])), ("S", self.sev_attn_s),
                          ("OAT", 16 * 16)]
                self.mm(self.psM[:, 512 * p:512 * (p + 1)],
                        self.pairsel[:, 128 * c:128 * (c + 1)],
                        self.aux1[0:16, 512 * p:512 * (p + 1)], waits=w0)
            t_rep = self.semv["T"]
            self.emit("vector", lambda e, cc=c: e.tensor_mul(chunk(cc), chunk(cc), self.psM[:, :]),
                      waits=[("T", t_rep)], inc="V")
            v_scaled[c] = self.semv["V"]

        v_ga, _ = self.replicate_mod(2, 0, clamp=True)
        ga_rep = self.rep[:, 0:1024]
        for c in range(NCH):
            self.emit("vector", lambda e, cc=c: e.tensor_mul(
                self.wqwo[:, D * cc:D * (cc + 1)], self.wqwo[:, D * cc:D * (cc + 1)], ga_rep),
                waits=[self.w("WO"), ("V", v_ga)] if c == 0 else [], inc="V")
        v_wofold = self.semv["V"]

        v_x1 = {}
        evac_done = {0: 0, 1: 0}
        for i in range(NT):
            half = i % 2
            for c in range(NCH):
                for p in range(2):
                    w0 = []
                    if c == 0 and p == 0:
                        w0 = [("V", max(v_wofold, v_scaled[NCH - 1], evac_done[half]))]
                    self.mm(self.psM[:, 1024 * half + 512 * p:1024 * half + 512 * (p + 1)],
                            chunk(c)[:, 128 * i:128 * (i + 1)],
                            self.wqwo[:, D * c + 512 * p:D * c + 512 * (p + 1)],
                            start=(c == 0), stop=(c == NCH - 1), waits=w0)
            t_wo = self.semv["T"]
            self.emit("vector",
                      lambda e, ii=i, hh=half: e.scalar_tensor_tensor(
                          xt(ii), self.psM[:, 1024 * hh:1024 * (hh + 1)], 1.0, xt(ii),
                          ALU.mult, ALU.add),
                      waits=[("T", t_wo)], inc="V")
            evac_done[half] = self.semv["V"]
            v_x1[i] = self.semv["V"]
        self.v_x1 = v_x1
        self.psm_last = {0: v_x1[NT - 2], 1: v_x1[NT - 1]}

        wf_view = self.h_wf.rearrange("(c p) i -> c p i", p=128)
        slot_t = [self.tq, self.tm]
        names = ["WFA", "WFB"]
        cnt = [0, 0]
        red_done = {}
        for c in range(NCH):
            slot = c % 2
            w0 = [("V", v_x1[NT - 1])] if c < 2 else [("V", red_done[c - 2])]
            self.dma("sync", slot_t[slot][:], wf_view[c], waits=w0, inc=names[slot])
            cnt[slot] += 1
            self.emit("vector", lambda e, ss=slot, cc=c: e.reduce_sum(
                self.stats[:, 4 * cc:4 * cc + 1], slot_t[ss][:], axis=AX,
                apply_absolute_value=True),
                      waits=[(names[slot], 16 * cnt[slot])], inc="V")
            red_done[c] = self.semv["V"]
        v_cs = self.semv["V"]
        self.mm(self.psA[0:1, 0:8], self.ones128f[:],
                self.stats.ap().rearrange("p (a b) -> p a b", b=4)[:, 0:8, 0:1],
                waits=[("V", v_cs)])
        t_ws = self.semv["T"]
        self.emit("vector", lambda e: e.reduce_sum(self.wsrow[0:1, 0:1], self.psA[0:1, 0:8], axis=AX),
                  waits=[("T", t_ws)], inc="V")
        v_ws0 = self.semv["V"]
        self.emit("scalar", lambda e: e.activation(self.wsrow[0:1, 4:5], self.wsrow[0:1, 0:1],
                                                   AF.Copy, bias=EPS_FFN, scale=1.0 / (D * D)),
                  waits=[("V", v_ws0)], inc="S")
        s_ws = self.semv["S"]
        self.mm(self.psA[:, 8:9], self.ones1f[:], self.wsrow[0:1, 4:5], waits=[("S", s_ws)])
        t_bc = self.semv["T"]
        self.emit("vector", lambda e: e.tensor_copy(self.smalls[:, 0:1], self.psA[:, 8:9]),
                  waits=[("T", t_bc)], inc="V")
        self.emit("vector", lambda e: e.reciprocal(self.smalls[:, 4:5], self.psA[:, 8:9]),
                  inc="V")
        self.emit("scalar", lambda e: e.activation(self.smalls[:, 8:9], self.psA[:, 8:9],
                                                   AF.Copy, scale=1.0 / 127.0),
                  waits=[("T", t_bc)], inc="S")
        self.v_wsbc = self.semv["V"]
        self.s_wsbc = self.semv["S"]

        wqT_view = self.bigA.ap().rearrange("p (ci g) -> p ci g", g=1024)
        wq_slots = [self.ckvT[:, 0:1024], self.ckvT[:, 1024:2048]]
        rws = self.smalls[:, 4:5]
        red2 = {}
        trw = 0
        for c in range(NCH):
            slot = c % 2
            w0 = [("V", self.v_wsbc)] if c < 2 else [("V", red2[c - 2])]
            self.dma("sync", slot_t[slot][:], wf_view[c], waits=w0, inc=names[slot])
            cnt[slot] += 1
            tsrc = slot_t[slot]
            w1 = [(names[slot], 16 * cnt[slot])]
            if c >= 2:
                w1.append(("TRW", 16 * (c - 1)))
            self.emit("vector", lambda e, t_=tsrc: e.tensor_scalar(t_[:], t_[:], rws, 128.0,
                                                                   ALU.mult, ALU.add),
                      waits=w1, inc="V")
            self.emit("vector", lambda e, t_=tsrc: e.tensor_copy(self.zi[:], t_[:]), inc="V")
            self.emit("vector", lambda e, t_=tsrc: e.scalar_tensor_tensor(t_[:], t_[:], -0.5, self.zi[:],
                                                                          ALU.add, ALU.is_ge), inc="V")
            self.emit("vector", lambda e, t_=tsrc, ss=slot: e.scalar_tensor_tensor(
                wq_slots[ss], self.zi[:], -128.0, t_[:], ALU.add, ALU.add), inc="V")
            self.emit("vector", lambda e, ss=slot: e.tensor_scalar(wq_slots[ss], wq_slots[ss],
                                                                   1.0, -1.0, ALU.min, ALU.max), inc="V")
            red2[c] = self.semv["V"]
            self.dma("scalar", wqT_view[:, :, 128 * c:128 * (c + 1)], wq_slots[slot],
                     waits=[("V", red2[c])], inc="TRW", transpose=True)
            trw += 1
        self.trw_total = trw

    def phase5(self):
        negm = lambda i: self.stats[:, 4 * i:4 * i + 1]
        varc = lambda i: self.stats[:, 64 + 4 * i:64 + 4 * i + 1]
        rstd = lambda i: self.stats[:, 128 + 4 * i:128 + 4 * i + 1]
        ffns = lambda i: self.stats[:, 192 + 4 * i:192 + 4 * i + 1]
        sqv = lambda i: self.stats[:, 256 + 4 * i:256 + 4 * i + 1]
        xt = lambda i: self.x_sb[:, D * i:D * (i + 1)]
        zT_view = self.bigB.ap().rearrange("p (j t) -> p j t", t=T)
        asc = self.smalls[:, 12:13]

        v_sb, _ = self.replicate_mod(3, 0)
        v_cb, _ = self.replicate_mod(4, 1)
        sb_rep = self.rep[:, 0:1024]
        cb_rep = self.rep[:, 1024:2048]

        trz = [0, 0]
        zinfo = {}
        for i in range(NT):
            slot = i % 2
            self.emit("vector", lambda e, ii=i: e.reduce_sum(varc(ii), xt(ii), axis=AX),
                      waits=[("V", self.v_x1[i])], inc="V")
            vm = self.semv["V"]
            self.emit("scalar", lambda e, ii=i: e.activation(negm(ii), varc(ii), AF.Copy,
                                                             scale=-1.0 / D),
                      waits=[("V", vm)], inc="S")
            sn = self.semv["S"]
            self.emit("vector", lambda e, ii=i: e.tensor_scalar(self.tq[:], xt(ii), negm(ii), None,
                                                                ALU.add),
                      waits=[("S", sn), ("TRW", 16 * self.trw_total)], inc="V")
            self.emit("vector", lambda e: e.tensor_mul(self.tm[:], self.tq[:], self.tq[:]), inc="V")
            self.emit("vector", lambda e, ii=i: e.reduce_sum(varc(ii), self.tm[:], axis=AX), inc="V")
            vv = self.semv["V"]
            self.emit("scalar", lambda e, ii=i: e.activation(sqv(ii), varc(ii), AF.Sqrt,
                                                             bias=EPS_LN, scale=1.0 / D),
                      waits=[("V", vv)], inc="S")
            sv = self.semv["S"]
            self.emit("vector", lambda e, ii=i: e.reciprocal(rstd(ii), sqv(ii)),
                      waits=[("S", sv)], inc="V")
            self.emit("vector", lambda e, ii=i: e.scalar_tensor_tensor(self.tq[:], xt(ii), negm(ii),
                                                                       cb_rep, ALU.add, ALU.mult),
                      waits=[("V", max(v_sb, v_cb))], inc="V")
            self.emit("vector", lambda e, ii=i: e.scalar_tensor_tensor(self.tm[:], self.tq[:], rstd(ii),
                                                                       sb_rep, ALU.mult, ALU.add), inc="V")
            self.emit("vector", lambda e: e.reduce_max(asc, self.tm[:], axis=AX,
                                                       apply_absolute_value=True), inc="V")
            v_asc = self.semv["V"]
            self.emit("scalar", lambda e: e.activation(self.smalls[:, 16:17], asc, AF.Copy, bias=EPS_FFN),
                      waits=[("V", v_asc), ("S", self.s_wsbc)], inc="S")
            self.emit("scalar", lambda e, ii=i: e.activation(ffns(ii), self.smalls[:, 16:17], AF.Copy,
                                                             scale=self.smalls[:, 8:9]), inc="S")
            s_asc = self.semv["S"]
            self.emit("vector", lambda e: e.reciprocal(self.smalls[:, 4:5], self.smalls[:, 16:17]),
                      waits=[("S", s_asc)], inc="V")
            v_rec = self.semv["V"]
            self.emit("scalar", lambda e: e.activation(self.smalls[:, 20:21], self.smalls[:, 4:5],
                                                       AF.Copy, scale=127.0),
                      waits=[("V", v_rec)], inc="S")
            s_sinv = self.semv["S"]
            ztile = self.h1b[:, D * slot:D * (slot + 1)]
            w0 = [(f"TRZ{slot}", 16 * trz[slot])] if i >= 2 else []
            w0.append(("S", s_sinv))
            self.emit("vector", lambda e: e.tensor_scalar(self.tq[:], self.tm[:], self.smalls[:, 20:21],
                                                          128.0, ALU.mult, ALU.add), waits=w0, inc="V")
            self.emit("vector", lambda e: e.tensor_scalar(self.tq[:], self.tq[:], 255.0, 0.0,
                                                          ALU.min, ALU.max), inc="V")
            self.emit("vector", lambda e: e.tensor_copy(self.zi[:], self.tq[:]), inc="V")
            self.emit("vector", lambda e: e.scalar_tensor_tensor(self.tm[:], self.tq[:], -0.5, self.zi[:],
                                                                 ALU.add, ALU.is_ge), inc="V")
            self.emit("vector", lambda e, z_=ztile: e.scalar_tensor_tensor(z_, self.zi[:], -128.0, self.tm[:],
                                                                           ALU.add, ALU.add), inc="V")
            v_z = self.semv["V"]
            self.dma("scalar", zT_view[:, :, 128 * i:128 * (i + 1)], ztile,
                     waits=[("V", v_z)], inc=f"TRZ{slot}", transpose=True)
            trz[slot] += 1
            zinfo[i] = (slot, trz[slot])

        v_gb, _ = self.replicate_mod(5, 0, clamp=True)
        gb_rep = self.rep[:, 0:1024]
        for ci in range(NCH):
            self.emit("vector", lambda e, cc=ci: e.tensor_mul(
                self.bigA[:, 1024 * cc:1024 * (cc + 1)], self.bigA[:, 1024 * cc:1024 * (cc + 1)],
                gb_rep),
                waits=[("V", v_gb), ("TRW", 16 * self.trw_total)] if ci == 0 else [], inc="V")
        v_gbfold = self.semv["V"]

        evac_done = dict(self.psm_last)
        for i in range(NT):
            half = i % 2
            slot, k = zinfo[i]
            for ci in range(NCH):
                for p in range(2):
                    w0 = []
                    if ci == 0 and p == 0:
                        w0 = [(f"TRZ{slot}", 16 * k), ("TRW", 16 * self.trw_total),
                              ("V", max(v_gbfold, evac_done[half]))]
                    self.mm(self.psM[:, 1024 * half + 512 * p:1024 * half + 512 * (p + 1)],
                            self.bigB[:, T * ci + 128 * i:T * ci + 128 * (i + 1)],
                            self.bigA[:, 1024 * ci + 512 * p:1024 * ci + 512 * (p + 1)],
                            start=(ci == 0), stop=(ci == NCH - 1), waits=w0)
            t_f = self.semv["T"]
            self.emit("vector",
                      lambda e, ii=i, hh=half: e.scalar_tensor_tensor(
                          xt(ii), self.psM[:, 1024 * hh:1024 * (hh + 1)],
                          ffns(ii), xt(ii), ALU.mult, ALU.add),
                      waits=[("T", t_f)], inc="V")
            evac_done[half] = self.semv["V"]
            v_out = self.semv["V"]
            self.dma("sync", self.h_out[128 * i:128 * (i + 1), :], xt(i),
                     waits=[("V", v_out)], inc="OUT")

    def finish(self):
        if self.stop is None:
            self.emit("sync", lambda e: e.nop(), waits=[("OUT", 16 * NT)])
        else:
            allw = [(s_, v) for s_, v in self.semv.items() if v > 0 and s_ != "OUT"]
            self.emit("sync", lambda e: e.nop(), waits=allw)
            for i in range(NT):
                self.dma("sync", self.h_out[128 * i:128 * (i + 1), :],
                         self.x_sb[:, D * i:D * (i + 1)], inc="OUT")
        if self.debug:
            for name, src in [("mod", self.mod_row), ("bigA", self.bigA),
                              ("bigB", self.bigB), ("ckvT", self.ckvT),
                              ("aux1", self.aux1), ("wmodb", self.wmodb),
                              ("stats", self.stats), ("smalls", self.smalls)]:
                self.dma("sync", self.h_dbg[name][:, :], src[:, :],
                         waits=[("OUT", self.semv["OUT"])], inc="OUT")
        self.emit("sync", lambda e: e.nop(), waits=[("OUT", self.semv["OUT"])])

    def finalize(self):
        nc = self.nc
        with nc.Block() as block:
            @block.sync
            def _(e):
                for f in self.ops["sync"]:
                    f(e)

            @block.gpsimd
            def _(e):
                for f in self.ops["gpsimd"]:
                    f(e)

            @block.tensor
            def _(e):
                for f in self.ops["tensor"]:
                    f(e)

            @block.vector
            def _(e):
                for f in self.ops["vector"]:
                    f(e)

            @block.scalar
            def _(e):
                for f in self.ops["scalar"]:
                    f(e)
        return nc


def make_consts():
    inv = 1.0 / 10000.0 ** (np.arange(0, DR, 2, dtype=np.float64) / DR)
    t = np.arange(T, dtype=np.float64)
    fr = np.outer(t, inv)
    cossin = np.concatenate([np.cos(fr).T, np.sin(fr).T], axis=0).astype(NPBF)
    tri = np.triu(np.ones((128, 128), np.float32)).astype(NPBF)
    pairsel = np.zeros((16, 1024), np.float32)
    for c in range(8):
        pairsel[2 * c, 128 * c:128 * c + 64] = 1.0
        pairsel[2 * c + 1, 128 * c + 64:128 * c + 128] = 1.0
    return cossin, tri, pairsel.astype(NPBF)


_CACHE = {}


def kernel(L0_attn_x, L0_attn_W_kv_down, L0_attn_W_k_up, L0_attn_W_v_up, L0_attn_W_q,
           L0_attn_W_q_rope, L0_attn_W_k_rope, L0_attn_W_o, L0_ffn_weight, L0_c, L0_wmod):
    if "nc" not in _CACHE:
        _CACHE["nc"] = Builder(debug=False).finalize()
    nc = _CACHE["nc"]
    cossin, tri, pairsel = make_consts()
    x = np.ascontiguousarray(np.asarray(L0_attn_x, dtype=np.float32))
    c = np.asarray(L0_c, dtype=np.float32)
    common = {
        "w_kv_down": np.ascontiguousarray(L0_attn_W_kv_down, dtype=np.float32),
        "w_k_up": np.ascontiguousarray(L0_attn_W_k_up, dtype=np.float32),
        "w_v_up": np.ascontiguousarray(L0_attn_W_v_up, dtype=np.float32),
        "w_q": np.ascontiguousarray(L0_attn_W_q, dtype=np.float32),
        "w_q_rope": np.ascontiguousarray(L0_attn_W_q_rope, dtype=np.float32),
        "w_k_rope": np.ascontiguousarray(L0_attn_W_k_rope, dtype=np.float32),
        "w_o": np.ascontiguousarray(L0_attn_W_o, dtype=np.float32),
        "w_ffn": np.ascontiguousarray(L0_ffn_weight, dtype=np.float32),
        "w_mod": np.ascontiguousarray(L0_wmod, dtype=np.float32),
        "cossin": cossin,
        "trimask": tri,
        "pairsel": pairsel,
    }
    in_maps = []
    for bidx in range(B):
        m = dict(common)
        m["x"] = np.ascontiguousarray(x[bidx])
        m["cT"] = np.ascontiguousarray(c[bidx].reshape(8, 128).T)
        in_maps.append(m)
    res = run_bass_kernel_spmd(nc, in_maps, core_ids=list(range(B)))
    return np.stack([r["out"] for r in res.results], axis=0)


if __name__ == "__main__":
    b = Builder(debug=False)
    nc = b.finalize()
    print("build ok; instructions:", {k: len(v) for k, v in b.ops.items()})


# version 06
# speedup vs baseline: 1.0237x; 1.0237x over previous
import numpy as np
import ml_dtypes
from contextlib import ExitStack

import concourse.bass as bass
import concourse.mybir as mybir
from concourse.bass_utils import run_bass_kernel_spmd

F32 = mybir.dt.float32
BF16 = mybir.dt.bfloat16
I32 = mybir.dt.int32
AF = mybir.ActivationFunctionType
ALU = mybir.AluOpType
AX = mybir.AxisListType.X

B, T, D, DC, DR, NH, HD = 8, 2048, 1024, 128, 32, 16, 64
NT = T // 128
NCH = D // 128
SCALE = float((HD + DR) ** -0.5)
SHIFT = 12.0
EPS_LN = 1e-5
EPS_FFN = 1e-8

NPBF = ml_dtypes.bfloat16


class Builder:
    ENGINES = ("sync", "gpsimd", "tensor", "vector", "scalar")

    def __init__(self, debug=False, stop=None):
        self.debug = debug
        self.stop = stop
        self.nc = bass.Bass("TRN2", target_bir_lowering=False)
        self.ops = {e: [] for e in self.ENGINES}
        self.semv = {}
        self.sems = {}
        self.ctx = ExitStack()
        self.lastw = {}
        self.build()

    def sem(self, name):
        self.sems[name] = self.ctx.enter_context(self.nc.semaphore(name))
        self.semv[name] = 0

    def w(self, name):
        return (name, self.semv[name])

    def emit(self, eng, fn, waits=(), inc=None, incn=1):
        waits = [(s, v) for (s, v) in waits if v > 0]
        for s, v in waits:
            assert s == "MODEV" or v <= self.semv[s], f"wait on {s}={v} > planned {self.semv[s]}"
        if inc is not None:
            self.semv[inc] += incn
        sems, lastw = self.sems, self.lastw

        def run(e, _fn=fn, _w=waits, _i=inc, _e=eng):
            for s, v in _w:
                k = (_e, s)
                if lastw.get(k, -1) < v:
                    e.wait_ge(sems[s], v)
                    lastw[k] = v
            r = _fn(e)
            if _i is not None:
                r.then_inc(sems[_i], incn)

        self.ops[eng].append(run)

    def dma(self, eng, out, in_, waits=(), inc=None, transpose=False):
        if transpose:
            self.emit(eng, lambda e, o=out, i=in_: e.dma_start_transpose(o, i),
                      waits=waits, inc=inc, incn=16)
        else:
            self.emit(eng, lambda e, o=out, i=in_: e.dma_start(o, i),
                      waits=waits, inc=inc, incn=16)

    def mm(self, out, lhsT, rhs, start=True, stop=True, waits=(), tp=None):
        self.emit("tensor",
                  lambda e, o=out, l=lhsT, r=rhs, s1=start, s2=stop, t=tp:
                  e.matmul(o, l, r, start=s1, stop=s2, tile_position=t),
                  waits=waits, inc="T")
        return self.semv["T"]

    def build(self):
        nc = self.nc
        E = self.ctx.enter_context
        ap = lambda t: t.ap()

        self.h_x = ap(nc.dram_tensor("x", [T, D], F32, kind="ExternalInput"))
        self.h_wkv = ap(nc.dram_tensor("w_kv_down", [D, DC], F32, kind="ExternalInput"))
        self.h_wku = ap(nc.dram_tensor("w_k_up", [DC, D], F32, kind="ExternalInput"))
        self.h_wvu = ap(nc.dram_tensor("w_v_up", [DC, D], F32, kind="ExternalInput"))
        self.h_wq = ap(nc.dram_tensor("w_q", [D, D], F32, kind="ExternalInput"))
        self.h_wqr = ap(nc.dram_tensor("w_q_rope", [D, DR], F32, kind="ExternalInput"))
        self.h_wkr = ap(nc.dram_tensor("w_k_rope", [DC, DR], F32, kind="ExternalInput"))
        self.h_wo = ap(nc.dram_tensor("w_o", [D, D], F32, kind="ExternalInput"))
        self.h_wf = ap(nc.dram_tensor("w_ffn", [D, D], F32, kind="ExternalInput"))
        self.h_wmod = ap(nc.dram_tensor("w_mod", [D, 6 * D], F32, kind="ExternalInput"))
        self.h_cT = ap(nc.dram_tensor("cT", [128, 8], F32, kind="ExternalInput"))
        self.h_cossin = ap(nc.dram_tensor("cossin", [32, T], BF16, kind="ExternalInput"))
        self.h_trimask = ap(nc.dram_tensor("trimask", [128, 128], BF16, kind="ExternalInput"))
        self.h_pairsel = ap(nc.dram_tensor("pairsel", [16, 1024], BF16, kind="ExternalInput"))
        self.h_out = ap(nc.dram_tensor("out", [T, D], F32, kind="ExternalOutput"))
        if self.debug:
            self.h_dbg = {}
            for name, shape, dt in [("mod", [1, 6 * D], BF16), ("bigA", [128, 16384], BF16),
                                    ("bigB", [128, 16384], BF16), ("ckvT", [128, T], BF16),
                                    ("aux1", [128, T], BF16), ("wmodb", [128, 2048], BF16),
                                    ("stats", [128, 320], F32), ("smalls", [128, 32], F32)]:
                self.h_dbg[name] = ap(nc.dram_tensor("dbg_" + name, shape, dt,
                                                     kind="ExternalOutput"))

        sb = lambda name, shape, dt: E(nc.sbuf_tensor(name, shape, dt))
        self.x_sb = sb("x_sb", [128, NT * D], F32)
        self.bigA = sb("bigA", [128, 16384], BF16)
        self.bigB = sb("bigB", [128, 16384], BF16)
        self.mod_row = sb("mod_row", [1, 6 * D], BF16)
        self.ckvT = sb("ckvT", [128, T], BF16)
        self.aux1 = sb("aux1", [128, T], BF16)
        self.cossin = sb("cossin_sb", [128, 2 * T], BF16)
        self.rep = sb("rep", [128, 2 * D], BF16)
        self.wmodb = sb("wmodb", [128, 2048], BF16)
        self.wqwo = sb("wqwo", [128, NCH * D], BF16)
        self.wkv = sb("wkv", [128, NCH * DC], BF16)
        self.wqr = sb("wqr", [128, NCH * DR], BF16)
        self.wku = sb("wku", [128, D], BF16)
        self.wvu = sb("wvu", [128, D], BF16)
        self.wkr = sb("wkr", [128, DR], BF16)
        self.tq = sb("tq", [128, D], F32)
        self.tm = sb("tm", [128, D], F32)
        self.h1b = sb("h1b", [128, 2 * D], BF16)
        self.zi = sb("zi", [128, D], I32)
        self.cpool = sb("cpool", [128, 12], F32)
        self.cT = sb("cT_sb", [128, 8], F32)
        self.cTb = sb("cTb", [128, 8], BF16)
        self.trimask = sb("trimask_sb", [128, 128], BF16)
        self.pairsel = sb("pairsel_sb", [16, 1024], BF16)
        self.ones1 = sb("ones1", [1, 128], BF16)
        self.ones1f = sb("ones1f", [1, 128], F32)
        self.ones128f = sb("ones128f", [128, 1], F32)
        self.stats = sb("stats", [128, 320], F32)
        self.smalls = sb("smalls", [128, 32], F32)
        self.wsrow = sb("wsrow", [1, 8], F32)

        self.consts = [-SHIFT, SCALE, 1.0, 1.0 / D, EPS_LN, 1.0 / (D * D), EPS_FFN,
                       -1.0 / D, 1.0 / 127.0, 127.0]
        for i, v in enumerate(self.consts):
            nc.const_aps.register(F32, float(v), self.cpool[:, i:i + 1])

        self.psA = E(nc.psum_tensor("psA", [128, 1024], F32))
        self.psB = E(nc.psum_tensor("psB", [128, 1024], F32))
        self.psM = E(nc.psum_tensor("psM", [128, 2048], F32))

        for s in ("T", "V", "S", "W", "WO", "CONST", "MODA", "MODB",
                  "SX0", "SX1", "SX2", "SX3", "TRH0", "TRH1", "TRZ0", "TRZ1",
                  "TRW", "WFA", "WFB", "OUT", "QPK", "OAT", "RCP0", "RCP1", "RP", "MODEV"):
            self.sem(s)

        order = ["p0", "p1", "p2", "p3", "p4", "p5"]
        fns = [self.phase0, self.phase1, self.phase2, self.phase3, self.phase4, self.phase5]
        for nm, fn in zip(order, fns):
            fn()
            if self.stop == nm:
                break
        self.finish()

    def phase0(self):
        def init(e):
            for i, v in enumerate(self.consts):
                e.memset(self.cpool[:, i:i + 1], float(v))
            e.memset(self.ones1[:], 1.0)
            e.memset(self.ones1f[:], 1.0)
            return e.memset(self.ones128f[:], 1.0)
        self.emit("vector", init, inc="V")
        self.v_init = self.semv["V"]

        self.dma("sync", self.cT[:], self.h_cT[:, :], inc="CONST")
        for q in range(4):
            self.dma("sync", self.x_sb.ap().rearrange("p (i d) -> p i d", d=D)[:, 4 * q:4 * q + 4, :],
                     self.h_x.rearrange("(i p) d -> p i d", p=128)[:, 4 * q:4 * q + 4, :],
                     inc=f"SX{q}")
        self.dma("sync", self.cossin[64:80, 0:T], self.h_cossin[0:16, :], inc="CONST")
        self.dma("sync", self.cossin[64:80, T:2 * T], self.h_cossin[16:32, :], inc="CONST")
        self.dma("sync", self.trimask[:], self.h_trimask[:, :], inc="CONST")
        self.dma("sync", self.pairsel[:], self.h_pairsel[:, :], inc="CONST")
        self.const_total = self.semv["CONST"]

        def emit_weight_loads():
            self.dma("gpsimd", self.wqwo.ap().rearrange("p (j o) -> p j o", o=D),
                     self.h_wq.rearrange("(j p) o -> p j o", p=128), inc="W")
            self.dma("gpsimd", self.wkv.ap().rearrange("p (j o) -> p j o", o=DC),
                     self.h_wkv.rearrange("(j p) o -> p j o", p=128), inc="W")
            self.dma("gpsimd", self.wqr.ap().rearrange("p (j o) -> p j o", o=DR),
                     self.h_wqr.rearrange("(j p) o -> p j o", p=128), inc="W")
            self.dma("gpsimd", self.wku[:], self.h_wku[:, :], inc="W")
            self.dma("gpsimd", self.wvu[:], self.h_wvu[:, :], inc="W")
            self.dma("gpsimd", self.wkr[:], self.h_wkr[:, :], inc="W")
            self.w_all = self.w("W")

        self.emit("scalar", lambda e: e.activation(self.cT[:], self.cT[:], AF.Silu),
                  waits=[("CONST", self.const_total)], inc="S")
        self.emit("scalar", lambda e: e.copy(self.cTb[:], self.cT[:]), inc="S")
        s_ctb = self.semv["S"]

        slots = [self.wmodb[:, 0:1024], self.wmodb[:, 1024:2048]]
        names = ["MODA", "MODB"]
        self.pending_mod_evacs = []
        mm_done = {}
        blk = 0
        for g in range(6):
            half = (g % 2) * 1024
            for j in range(8):
                slot = blk % 2
                dwaits = [("T", mm_done[blk - 2])] if blk >= 2 else []
                self.dma("gpsimd", slots[slot],
                         self.h_wmod[128 * j:128 * (j + 1), 1024 * g:1024 * (g + 1)],
                         waits=dwaits, inc=names[slot])
                mwaits = [(names[slot], 16 * (blk // 2 + 1)), ("S", s_ctb)]
                if g >= 2 and j == 0:
                    mwaits.append(("MODEV", g - 1))
                for p in range(2):
                    self.mm(self.psM[0:1, half + 512 * p: half + 512 * (p + 1)],
                            self.cTb[:, j:j + 1], slots[slot][:, 512 * p:512 * (p + 1)],
                            start=(j == 0), stop=(j == 7),
                            waits=mwaits if p == 0 else [])
                mm_done[blk] = self.semv["T"]
                blk += 1
                if blk == 8:
                    emit_weight_loads()
            tg = self.semv["T"]
            if g < 2:
                self._emit_one_mod_evac(g, half, tg)
            else:
                self.pending_mod_evacs.append((g, half, tg))
        self.mod_stream_done = [("MODA", 16 * 24), ("MODB", 16 * 24)]

    def _emit_one_mod_evac(self, g, half, tg):
        dst = self.mod_row[0:1, 1024 * g:1024 * (g + 1)]
        if g in (1, 4):
            self.emit("scalar", lambda e, d=dst, hh=half: e.activation(
                d, self.psM[0:1, hh:hh + 1024], AF.Copy, bias=1.0),
                waits=[("T", tg)], inc="MODEV")
        else:
            self.emit("scalar", lambda e, d=dst, hh=half: e.copy(
                d, self.psM[0:1, hh:hh + 1024]),
                waits=[("T", tg)], inc="MODEV")

    def emit_mod_evacs(self, upto):
        while self.pending_mod_evacs and self.pending_mod_evacs[0][0] < upto:
            g, half, tg = self.pending_mod_evacs.pop(0)
            self._emit_one_mod_evac(g, half, tg)

    def replicate_mod(self, vec_idx, slot, clamp=False):
        ps = self.psA
        mmw = [("MODEV", vec_idx + 1), self.w("V"), self.w("S")]
        for p in range(2):
            self.mm(ps[:, 512 * p:512 * (p + 1)], self.ones1[:],
                    self.mod_row[0:1, 1024 * vec_idx + 512 * p:1024 * vec_idx + 512 * (p + 1)],
                    waits=mmw if p == 0 else [])
        t_done = self.semv["T"]
        dst = self.rep[:, 1024 * slot:1024 * (slot + 1)]
        if not clamp:
            self.emit("vector", lambda e: e.tensor_copy(dst, ps[:, 0:1024]),
                      waits=[("T", t_done)], inc="V")
        else:
            tq, tm = self.tq, self.tm
            self.emit("vector", lambda e: e.tensor_scalar(tq[:], ps[:, 0:1024], 0.0, None, ALU.is_ge),
                      waits=[("T", t_done)], inc="V")
            self.emit("vector", lambda e: e.tensor_scalar(tm[:], ps[:, 0:1024], 0.25, None, ALU.max), inc="V")
            self.emit("vector", lambda e: e.tensor_scalar(dst, ps[:, 0:1024], -0.25, None, ALU.min), inc="V")
            self.emit("vector", lambda e: e.tensor_sub(tm[:], tm[:], dst), inc="V")
            self.emit("vector", lambda e: e.tensor_mul(tm[:], tm[:], tq[:]), inc="V")
            self.emit("vector", lambda e: e.tensor_add(dst, dst, tm[:]), inc="V")
        return self.semv["V"], t_done

    def phase1(self):
        sqv = lambda i: self.stats[:, 256 + 4 * i:256 + 4 * i + 1]
        negm = lambda i: self.stats[:, 4 * i:4 * i + 1]
        varc = lambda i: self.stats[:, 64 + 4 * i:64 + 4 * i + 1]
        rstd = lambda i: self.stats[:, 128 + 4 * i:128 + 4 * i + 1]
        xt = lambda i: self.x_sb[:, D * i:D * (i + 1)]

        self.sev_rstd = {}
        for i in range(NT):
            self.emit("vector", lambda e, ii=i: e.reduce_sum(varc(ii), xt(ii), axis=AX),
                      waits=[(f"SX{i // 4}", 16), ("V", self.v_init)], inc="V")
            vm = self.semv["V"]
            self.emit("scalar", lambda e, ii=i: e.activation(negm(ii), varc(ii), AF.Copy,
                                                             scale=-1.0 / D),
                      waits=[("V", vm)], inc="S")
            sn = self.semv["S"]
            self.emit("vector", lambda e, ii=i: e.tensor_scalar(self.tq[:], xt(ii), negm(ii), None,
                                                                ALU.add),
                      waits=[("S", sn)], inc="V")
            self.emit("vector", lambda e: e.tensor_mul(self.tm[:], self.tq[:], self.tq[:]), inc="V")
            self.emit("vector", lambda e, ii=i: e.reduce_sum(varc(ii), self.tm[:], axis=AX), inc="V")
            vv = self.semv["V"]
            self.emit("scalar", lambda e, ii=i: e.activation(sqv(ii), varc(ii), AF.Sqrt,
                                                             bias=EPS_LN, scale=1.0 / D),
                      waits=[("V", vv)], inc="S")
            sv = self.semv["S"]
            self.emit("vector", lambda e, ii=i: e.reciprocal(rstd(ii), sqv(ii)),
                      waits=[("S", sv)], inc="V")
            self.sev_rstd[i] = self.semv["V"]

        self.emit_mod_evacs(6)
        v_sa, _ = self.replicate_mod(0, 0)
        v_ca, self.t_carep = self.replicate_mod(1, 1)
        sa_rep = self.rep[:, 0:1024]
        ca_rep = self.rep[:, 1024:2048]

        h1T_dst = self.bigA.ap().rearrange("p (j t) -> p j t", t=T)
        trh = [0, 0]
        for i in range(NT):
            slot = i % 2
            h1tile = self.h1b[:, D * slot:D * (slot + 1)]
            waits = [("V", max(v_sa, v_ca, self.sev_rstd[i]))]
            if i >= 2:
                waits.append((f"TRH{slot}", 16 * trh[slot]))
            self.emit("vector",
                      lambda e, ii=i: e.scalar_tensor_tensor(self.tm[:], xt(ii), negm(ii), ca_rep,
                                                             ALU.add, ALU.mult),
                      waits=waits, inc="V")
            self.emit("vector",
                      lambda e, ii=i, h1=h1tile: e.scalar_tensor_tensor(h1, self.tm[:], rstd(ii), sa_rep,
                                                                        ALU.mult, ALU.add),
                      inc="V")
            vap = self.semv["V"]
            self.dma("scalar", h1T_dst[:, :, 128 * i:128 * (i + 1)], h1tile,
                     waits=[("V", vap)], inc=f"TRH{slot}", transpose=True)
            trh[slot] += 1
        self.p1_done_w = [("TRH0", 16 * trh[0]), ("TRH1", 16 * trh[1])]

    def phase2(self):
        h1T = lambda j: self.bigA[:, T * j:T * (j + 1)]

        for p in range(4):
            for j in range(NCH):
                w0 = []
                if p == 0 and j == 0:
                    w0 = self.p1_done_w + [self.w_all, ("MODEV", 6)]
                self.mm(self.psM[:, 512 * p:512 * (p + 1)],
                        self.wkv[:, DC * j:DC * (j + 1)],
                        h1T(j)[:, 512 * p:512 * (p + 1)],
                        start=(j == 0), stop=(j == NCH - 1), waits=w0)
        t_ckv = self.semv["T"]
        self.emit("scalar", lambda e: e.copy(self.ckvT[:], self.psM[:, 0:2048]),
                  waits=[("T", t_ckv)], inc="S")
        s_ckv = self.semv["S"]

        def rope(dst, stat_x1, stat_x2, nacc, mov_of, extra_first=(), extra_v=()):
            cosv = self.cossin[64:80, :]
            first = list(extra_first)
            v_last = 0
            for q in range(4):
                lo = 512 * q
                ps = self.psA if q % 2 == 0 else self.psB
                for j in range(nacc):
                    self.mm(ps[64:80, 0:512], stat_x1(j), mov_of(j, lo, 512),
                            start=(j == 0), stop=(j == nacc - 1),
                            waits=(first + ([("V", v_last)] if q >= 2 else [])) if j == 0 else [],
                            tp=(0, 64))
                    first = []
                    self.mm(ps[64:80, 512:1024], stat_x2(j), mov_of(j, lo, 512),
                            start=(j == 0), stop=(j == nacc - 1), tp=(0, 64))
                t_mm = self.semv["T"]
                x1 = ps[64:80, 0:512]
                x2 = ps[64:80, 512:1024]
                cosp = cosv[:, lo:lo + 512]
                sinp = cosv[:, T + lo:T + lo + 512]
                th = self.tq[64:80, 0:512]
                uh = self.tq[64:80, 512:1024]
                self.emit("vector", lambda e, a=x1, b=cosp, c=th: e.tensor_mul(c, a, b),
                          waits=[("T", t_mm), ("CONST", self.const_total)] + list(extra_v), inc="V")
                extra_v = ()
                self.emit("vector", lambda e, a=x2, b=sinp, c=uh: e.tensor_mul(c, a, b), inc="V")
                self.emit("vector", lambda e, a=th, b=uh, c=dst[64:80, lo:lo + 512]:
                          e.tensor_sub(c, a, b), inc="V")
                self.emit("vector", lambda e, a=x2, b=cosp, c=th: e.tensor_mul(c, a, b), inc="V")
                self.emit("vector", lambda e, a=x1, b=sinp, c=uh: e.tensor_mul(c, a, b), inc="V")
                self.emit("vector", lambda e, a=th, b=uh, c=self.h1b[64:80, lo:lo + 512]:
                          e.tensor_add(c, a, b), inc="V")
                v_last = self.semv["V"]
            self.dma("gpsimd", dst[96:112, 0:T], self.h1b[64:80, 0:T],
                     waits=[("V", v_last)] + self.p1_done_w, inc="RP")
            return v_last

        rope(self.wmodb,
             lambda j: self.wkr[:, 0:16], lambda j: self.wkr[:, 16:32], 1,
             lambda j, lo, w: self.ckvT[:, lo:lo + w],
             extra_first=[("S", s_ckv), self.w("V")] + self.mod_stream_done)

        s_prev_evac = s_ckv
        for m in range(NCH):
            for j in range(NCH):
                for p in range(4):
                    w0 = []
                    if j == 0 and p == 0:
                        w0 = [("S", s_prev_evac)]
                    self.mm(self.psM[:, 512 * p:512 * (p + 1)],
                            self.wqwo[:, D * j + 128 * m:D * j + 128 * (m + 1)],
                            h1T(j)[:, 512 * p:512 * (p + 1)],
                            start=(j == 0), stop=(j == NCH - 1), waits=w0)
            t_q = self.semv["T"]
            self.emit("scalar", lambda e, mm_=m: e.copy(self.bigB[:, T * mm_:T * (mm_ + 1)],
                                                        self.psM[:, 0:2048]),
                      waits=[("T", t_q)], inc="S")
            s_prev_evac = self.semv["S"]
        self.sev_qT = self.semv["S"]

        v_qr = rope(self.aux1,
                    lambda j: self.wqr[:, DR * j:DR * j + 16],
                    lambda j: self.wqr[:, DR * j + 16:DR * j + 32], NCH,
                    lambda j, lo, w: h1T(j)[:, lo:lo + w], extra_v=[("RP", 16)])
        self.sev_rope = v_qr
        self.t_h1T_done = self.semv["T"]
        self.rp_total = self.semv["RP"]

    def phase3(self):
        KH = [self.bigA[:, 0:T], self.bigA[:, T:2 * T]]
        QH = [self.bigA[:, 2 * T:3 * T], self.bigA[:, 3 * T:4 * T]]
        VH = [self.bigA[:, 8192:8192 + 1040], self.bigA[:, 9232:9232 + 1040]]
        STG = [self.bigA[:, 10272 + 1024 * s:10272 + 1024 * (s + 1)] for s in range(4)]
        chunk = lambda c: self.bigB[:, T * c:T * (c + 1)]

        w_hd = [("T", self.t_h1T_done), ("RP", self.rp_total)]
        for pi in range(2):
            self.emit("vector", lambda e, b=KH[pi]: e.memset(b[64:128, :], 0.0),
                      waits=w_hd, inc="V")
            w_hd = []
            self.emit("vector", lambda e, b=QH[pi]: e.memset(b[64:128, :], 0.0), inc="V")
            self.emit("vector", lambda e, b=KH[pi]: e.tensor_copy(b[64:80, :], self.wmodb[64:80, :]),
                      inc="V")
            self.emit("vector", lambda e, b=KH[pi]: e.tensor_copy(b[96:112, :], self.wmodb[96:112, :]),
                      inc="V")
            self.emit("vector", lambda e, b=QH[pi]: e.tensor_copy(b[64:80, :], self.aux1[64:80, :]),
                      inc="V")
            self.emit("vector", lambda e, b=QH[pi]: e.tensor_copy(b[96:112, :], self.aux1[96:112, :]),
                      inc="V")
        self.emit("vector", lambda e: e.memset(VH[0], 1.0), inc="V")
        self.emit("vector", lambda e: e.memset(VH[1], 1.0), inc="V")
        v_ones = self.semv["V"]

        exp_done_ps = {}
        self.recip_rows = 0
        qpk = [0]

        scores_done_prev = 0
        av_done_prev = 0
        v_prep = v_ones
        v_half1_evac_prev = 0

        def prep_stageA(h, sc_prev, psm_ok):
            pi = h % 2
            for p in range(2):
                w0 = list(psm_ok) if p == 0 else []
                self.mm(self.psM[0:64, 512 * p:512 * (p + 1)],
                        self.wku[:, HD * h:HD * (h + 1)],
                        self.ckvT[:, 512 * p:512 * (p + 1)], waits=w0)
            t1 = self.semv["T"]
            self.emit("vector", lambda e, pi_=pi: e.tensor_copy(KH[pi_][0:64, 0:1024],
                                                                self.psM[0:64, 0:1024]),
                      waits=[("T", max(t1, sc_prev))], inc="V")
            return self.semv["V"]

        def prep_stageB(h, v1):
            pi = h % 2
            for p in range(2):
                w0 = [("V", v1)] if p == 0 else []
                self.mm(self.psM[0:64, 512 * p:512 * (p + 1)],
                        self.wku[:, HD * h:HD * (h + 1)],
                        self.ckvT[:, 1024 + 512 * p:1024 + 512 * (p + 1)], waits=w0)
            t2 = self.semv["T"]
            self.emit("vector", lambda e, pi_=pi: e.tensor_copy(KH[pi_][0:64, 1024:2048],
                                                                self.psM[0:64, 0:1024]),
                      waits=[("T", t2)], inc="V")
            return self.semv["V"]

        def prep_stageC(h, av_prev, v2):
            pi = h % 2
            for j in range(NT):
                w0 = [("V", v2)] if j == 0 else []
                self.mm(self.psM[:, 64 * j:64 * (j + 1)],
                        self.ckvT[:, 128 * j:128 * (j + 1)],
                        self.wvu[:, HD * h:HD * (h + 1)], waits=w0)
            t_vup = self.semv["T"]
            self.emit("vector",
                      lambda e, pi_=pi: e.tensor_copy(
                          VH[pi_].rearrange("p (j g) -> p j g", g=65)[:, :, 0:64],
                          self.psM.ap().rearrange("p (j g) -> p j g", g=64)[:, 0:16, :]),
                      waits=[("T", max(t_vup, av_prev)), ("V", v_ones)], inc="V")
            if h % 2 == 0:
                self.emit("vector", lambda e, hh=h, pi_=pi: e.tensor_copy(
                    QH[pi_][0:64, :], chunk(hh // 2)[0:64, :]),
                    waits=[("S", self.sev_qT)], inc="V")
            else:
                self.dma("sync", QH[pi][0:64, :], chunk(h // 2)[64:128, :],
                         waits=[("S", self.sev_qT)], inc="QPK")
                qpk[0] += 1
            return self.semv["V"], qpk[0]

        def emit_prep(h, sc_prev, av_prev, psab_free, psm_ok):
            v1 = prep_stageA(h, sc_prev, psm_ok)
            v2 = prep_stageB(h, v1)
            return prep_stageC(h, av_prev, v2)

        v_prep, q_prep = emit_prep(0, 0, 0, 0, [self.w("S"), self.w("V"), ("V", self.sev_rope)])

        for h in range(NH):
            pi = h % 2
            blkctr = 0
            binfo_of_j = {}
            first_w = [("V", v_prep)]
            if pi == 1:
                first_w.append(("QPK", 16 * q_prep))
            mid_done = {}

            def do_scores(j):
                nonlocal blkctr, first_w
                q0 = 128 * j
                binfo = []
                bstart = q0
                while bstart < T:
                    wblk = min(1024, T - bstart)
                    par = blkctr % 2
                    ps = self.psA if par == 0 else self.psB
                    stg = blkctr % 4
                    w0 = first_w
                    first_w = []
                    if par in exp_done_ps:
                        w0 = w0 + [("S", exp_done_ps[par])]
                    off = 0
                    while off < wblk:
                        pw = min(512, wblk - off)
                        self.mm(ps[:, off:off + pw],
                                KH[pi][:, 128 * j:128 * (j + 1)],
                                QH[pi][:, bstart + off:bstart + off + pw], waits=w0)
                        w0 = []
                        off += 512
                    t_mm = self.semv["T"]
                    self.emit("scalar",
                              lambda e, ps_=ps, w_=wblk, s_=stg: e.activation(
                                  STG[s_][:, 0:w_], ps_[:, 0:w_], AF.Exp,
                                  bias=-SHIFT, scale=SCALE),
                              waits=[("T", t_mm)], inc="S")
                    exp_done_ps[par] = self.semv["S"]
                    s_exp = self.semv["S"]
                    v_mask = 0
                    if bstart == q0:
                        self.emit("vector",
                                  lambda e, s_=stg: e.tensor_mul(
                                      STG[s_][:, 0:128], STG[s_][:, 0:128], self.trimask[:]),
                                  waits=[("S", s_exp)], inc="V")
                        v_mask = self.semv["V"]
                    binfo.append((bstart, wblk, stg, s_exp, v_mask))
                    blkctr += 1
                    bstart += wblk
                binfo_of_j[j] = binfo

            def do_av(j, half1_guard):
                for (bstart, wblk, stg, s_exp, v_mask) in binfo_of_j.pop(j):
                    s0 = bstart
                    while s0 < bstart + wblk:
                        s1 = min((s0 // 512 + 1) * 512, bstart + wblk)
                        gw = s1 - s0
                        reg_last = min(NT - 1, (s1 - 1) // 128)
                        w0 = [("S", s_exp)]
                        if v_mask:
                            w0.append(("V", v_mask))
                        if j == 0 and half1_guard:
                            w0.append(("V", half1_guard))
                        self.mm(self.psM[0:65, s0:s1],
                                VH[pi][:, 65 * j:65 * j + 65],
                                STG[stg][:, s0 - bstart:s0 - bstart + gw],
                                start=(j == 0), stop=(j == reg_last), waits=w0)
                        s0 = s1

            def evac_attn_half(h_, half, t_guard):
                pi_ = h_ % 2
                lo = 1024 * half
                dst_rows = (64 * (h_ % 2), 64 * (h_ % 2) + 64)
                if h_ % 2 == 0:
                    self.emit("vector", lambda e, hh=h_, l=lo: e.tensor_copy(
                        chunk(hh // 2)[0:64, l:l + 1024], self.psM[0:64, l:l + 1024]),
                        waits=[("T", t_guard)], inc="V")
                else:
                    self.emit("vector", lambda e, hh=h_, l=lo: e.tensor_copy(
                        self.h1b[0:64, l:l + 1024], self.psM[0:64, l:l + 1024]),
                        waits=[("T", t_guard), ("OAT", 16 * (2 * (h_ // 2) + half))], inc="V")
                    v_st = self.semv["V"]
                    self.dma("sync", chunk(h_ // 2)[64:128, lo:lo + 1024],
                             self.h1b[0:64, lo:lo + 1024],
                             waits=[("V", v_st)], inc="OAT")
                dslot = 64 + 32 * (h_ % 2)
                self.emit("vector", lambda e, l=lo, ds=dslot: e.tensor_copy(
                    self.h1b[ds:ds + 1, l:l + 1024], self.psM[64:65, l:l + 1024]),
                    waits=[("T", t_guard), (f"RCP{h_ % 2}", 16 * (h_ // 2))], inc="V")
                return self.semv["V"]

            pst = {}
            for j in range(NT):
                do_scores(j)
                if j >= 1:
                    do_av(j - 1, v_half1_evac_prev)
                if j == 8:
                    t_av7 = self.semv["T"]
                    pst["vh0"] = evac_attn_half(h, 0, t_av7)
                elif j == 10 and h + 1 < NH:
                    pst["v1"] = prep_stageA(h + 1, scores_done_prev, [("V", pst["vh0"])])
                elif j == 12 and h + 1 < NH:
                    pst["v2"] = prep_stageB(h + 1, pst["v1"])
                elif j == 14 and h + 1 < NH:
                    v_prep, q_prep = prep_stageC(h + 1, av_done_prev, pst["v2"])
            scores_done_cur = self.semv["T"]
            do_av(NT - 1, v_half1_evac_prev)
            t_av_all = self.semv["T"]
            v_h1 = evac_attn_half(h, 1, t_av_all)
            v_half1_evac_prev = v_h1
            dslot = 64 + 32 * (h % 2)
            self.dma("gpsimd", self.aux1[h:h + 1, :], self.h1b[dslot:dslot + 1, 0:2048],
                     waits=[("V", v_h1)], inc=f"RCP{h % 2}")
            scores_done_prev = scores_done_cur
            av_done_prev = t_av_all

        self.sev_attn_s = self.semv["S"]
        self.emit("vector", lambda e: e.reciprocal(self.tq[0:16, 0:1024], self.aux1[0:16, 0:1024]),
                  waits=[("RCP0", 16 * 8), ("RCP1", 16 * 8)], inc="V")
        self.emit("vector", lambda e: e.reciprocal(self.tm[0:16, 0:1024], self.aux1[0:16, 1024:2048]),
                  inc="V")
        self.emit("vector", lambda e: e.tensor_copy(self.aux1[0:16, 0:1024], self.tq[0:16, 0:1024]),
                  inc="V")
        self.emit("vector", lambda e: e.tensor_copy(self.aux1[0:16, 1024:2048], self.tm[0:16, 0:1024]),
                  inc="V")
        self.v_recip = self.semv["V"]

    def phase4(self):
        chunk = lambda c: self.bigB[:, T * c:T * (c + 1)]
        xt = lambda i: self.x_sb[:, D * i:D * (i + 1)]

        self.dma("gpsimd", self.wqwo.ap().rearrange("p (j o) -> p j o", o=D),
                 self.h_wo.rearrange("(j p) o -> p j o", p=128),
                 waits=[("S", self.sev_qT)], inc="WO")

        v_scaled = {-1: 0}
        for c in range(NCH):
            for p in range(4):
                w0 = []
                if p == 0:
                    w0 = [("V", max(self.v_recip, v_scaled[c - 1])), ("S", self.sev_attn_s),
                          ("OAT", 16 * 16)]
                self.mm(self.psM[:, 512 * p:512 * (p + 1)],
                        self.pairsel[:, 128 * c:128 * (c + 1)],
                        self.aux1[0:16, 512 * p:512 * (p + 1)], waits=w0)
            t_rep = self.semv["T"]
            self.emit("vector", lambda e, cc=c: e.tensor_mul(chunk(cc), chunk(cc), self.psM[:, :]),
                      waits=[("T", t_rep)], inc="V")
            v_scaled[c] = self.semv["V"]

        v_ga, _ = self.replicate_mod(2, 0, clamp=True)
        ga_rep = self.rep[:, 0:1024]
        for c in range(NCH):
            self.emit("vector", lambda e, cc=c: e.tensor_mul(
                self.wqwo[:, D * cc:D * (cc + 1)], self.wqwo[:, D * cc:D * (cc + 1)], ga_rep),
                waits=[self.w("WO"), ("V", v_ga)] if c == 0 else [], inc="V")
        v_wofold = self.semv["V"]

        v_x1 = {}
        evac_done = {0: 0, 1: 0}
        for i in range(NT):
            half = i % 2
            for c in range(NCH):
                for p in range(2):
                    w0 = []
                    if c == 0 and p == 0:
                        w0 = [("V", max(v_wofold, v_scaled[NCH - 1], evac_done[half]))]
                    self.mm(self.psM[:, 1024 * half + 512 * p:1024 * half + 512 * (p + 1)],
                            chunk(c)[:, 128 * i:128 * (i + 1)],
                            self.wqwo[:, D * c + 512 * p:D * c + 512 * (p + 1)],
                            start=(c == 0), stop=(c == NCH - 1), waits=w0)
            t_wo = self.semv["T"]
            self.emit("vector",
                      lambda e, ii=i, hh=half: e.scalar_tensor_tensor(
                          xt(ii), self.psM[:, 1024 * hh:1024 * (hh + 1)], 1.0, xt(ii),
                          ALU.mult, ALU.add),
                      waits=[("T", t_wo)], inc="V")
            evac_done[half] = self.semv["V"]
            v_x1[i] = self.semv["V"]
        self.v_x1 = v_x1
        self.psm_last = {0: v_x1[NT - 2], 1: v_x1[NT - 1]}

        wf_view = self.h_wf.rearrange("(c p) i -> c p i", p=128)
        slot_t = [self.tq, self.tm]
        names = ["WFA", "WFB"]
        cnt = [0, 0]
        red_done = {}
        for c in range(NCH):
            slot = c % 2
            w0 = [("V", v_x1[NT - 1])] if c < 2 else [("V", red_done[c - 2])]
            self.dma("sync", slot_t[slot][:], wf_view[c], waits=w0, inc=names[slot])
            cnt[slot] += 1
            self.emit("vector", lambda e, ss=slot, cc=c: e.reduce_sum(
                self.stats[:, 4 * cc:4 * cc + 1], slot_t[ss][:], axis=AX,
                apply_absolute_value=True),
                      waits=[(names[slot], 16 * cnt[slot])], inc="V")
            red_done[c] = self.semv["V"]
        v_cs = self.semv["V"]
        self.mm(self.psA[0:1, 0:8], self.ones128f[:],
                self.stats.ap().rearrange("p (a b) -> p a b", b=4)[:, 0:8, 0:1],
                waits=[("V", v_cs)])
        t_ws = self.semv["T"]
        self.emit("vector", lambda e: e.reduce_sum(self.wsrow[0:1, 0:1], self.psA[0:1, 0:8], axis=AX),
                  waits=[("T", t_ws)], inc="V")
        v_ws0 = self.semv["V"]
        self.emit("scalar", lambda e: e.activation(self.wsrow[0:1, 4:5], self.wsrow[0:1, 0:1],
                                                   AF.Copy, bias=EPS_FFN, scale=1.0 / (D * D)),
                  waits=[("V", v_ws0)], inc="S")
        s_ws = self.semv["S"]
        self.mm(self.psA[:, 8:9], self.ones1f[:], self.wsrow[0:1, 4:5], waits=[("S", s_ws)])
        t_bc = self.semv["T"]
        self.emit("vector", lambda e: e.tensor_copy(self.smalls[:, 0:1], self.psA[:, 8:9]),
                  waits=[("T", t_bc)], inc="V")
        self.emit("vector", lambda e: e.reciprocal(self.smalls[:, 4:5], self.psA[:, 8:9]),
                  inc="V")
        self.emit("scalar", lambda e: e.activation(self.smalls[:, 8:9], self.psA[:, 8:9],
                                                   AF.Copy, scale=1.0 / 127.0),
                  waits=[("T", t_bc)], inc="S")
        self.v_wsbc = self.semv["V"]
        self.s_wsbc = self.semv["S"]

        wqT_view = self.bigA.ap().rearrange("p (ci g) -> p ci g", g=1024)
        wq_slots = [self.ckvT[:, 0:1024], self.ckvT[:, 1024:2048]]
        rws = self.smalls[:, 4:5]
        red2 = {}
        trw = 0
        for c in range(NCH):
            slot = c % 2
            w0 = [("V", self.v_wsbc)] if c < 2 else [("V", red2[c - 2])]
            self.dma("sync", slot_t[slot][:], wf_view[c], waits=w0, inc=names[slot])
            cnt[slot] += 1
            tsrc = slot_t[slot]
            w1 = [(names[slot], 16 * cnt[slot])]
            if c >= 2:
                w1.append(("TRW", 16 * (c - 1)))
            self.emit("vector", lambda e, t_=tsrc: e.tensor_scalar(t_[:], t_[:], rws, 128.0,
                                                                   ALU.mult, ALU.add),
                      waits=w1, inc="V")
            self.emit("vector", lambda e, t_=tsrc: e.tensor_copy(self.zi[:], t_[:]), inc="V")
            self.emit("vector", lambda e, t_=tsrc: e.scalar_tensor_tensor(t_[:], t_[:], -0.5, self.zi[:],
                                                                          ALU.add, ALU.is_ge), inc="V")
            self.emit("vector", lambda e, t_=tsrc, ss=slot: e.scalar_tensor_tensor(
                wq_slots[ss], self.zi[:], -128.0, t_[:], ALU.add, ALU.add), inc="V")
            self.emit("vector", lambda e, ss=slot: e.tensor_scalar(wq_slots[ss], wq_slots[ss],
                                                                   1.0, -1.0, ALU.min, ALU.max), inc="V")
            red2[c] = self.semv["V"]
            self.dma("scalar", wqT_view[:, :, 128 * c:128 * (c + 1)], wq_slots[slot],
                     waits=[("V", red2[c])], inc="TRW", transpose=True)
            trw += 1
        self.trw_total = trw

    def phase5(self):
        negm = lambda i: self.stats[:, 4 * i:4 * i + 1]
        varc = lambda i: self.stats[:, 64 + 4 * i:64 + 4 * i + 1]
        rstd = lambda i: self.stats[:, 128 + 4 * i:128 + 4 * i + 1]
        ffns = lambda i: self.stats[:, 192 + 4 * i:192 + 4 * i + 1]
        sqv = lambda i: self.stats[:, 256 + 4 * i:256 + 4 * i + 1]
        xt = lambda i: self.x_sb[:, D * i:D * (i + 1)]
        zT_view = self.bigB.ap().rearrange("p (j t) -> p j t", t=T)
        asc = self.smalls[:, 12:13]

        v_sb, _ = self.replicate_mod(3, 0)
        v_cb, _ = self.replicate_mod(4, 1)
        sb_rep = self.rep[:, 0:1024]
        cb_rep = self.rep[:, 1024:2048]

        trz = [0, 0]
        zinfo = {}
        for i in range(NT):
            slot = i % 2
            self.emit("vector", lambda e, ii=i: e.reduce_sum(varc(ii), xt(ii), axis=AX),
                      waits=[("V", self.v_x1[i])], inc="V")
            vm = self.semv["V"]
            self.emit("scalar", lambda e, ii=i: e.activation(negm(ii), varc(ii), AF.Copy,
                                                             scale=-1.0 / D),
                      waits=[("V", vm)], inc="S")
            sn = self.semv["S"]
            self.emit("vector", lambda e, ii=i: e.tensor_scalar(self.tq[:], xt(ii), negm(ii), None,
                                                                ALU.add),
                      waits=[("S", sn), ("TRW", 16 * self.trw_total)], inc="V")
            self.emit("vector", lambda e: e.tensor_mul(self.tm[:], self.tq[:], self.tq[:]), inc="V")
            self.emit("vector", lambda e, ii=i: e.reduce_sum(varc(ii), self.tm[:], axis=AX), inc="V")
            vv = self.semv["V"]
            self.emit("scalar", lambda e, ii=i: e.activation(sqv(ii), varc(ii), AF.Sqrt,
                                                             bias=EPS_LN, scale=1.0 / D),
                      waits=[("V", vv)], inc="S")
            sv = self.semv["S"]
            self.emit("vector", lambda e, ii=i: e.reciprocal(rstd(ii), sqv(ii)),
                      waits=[("S", sv)], inc="V")
            self.emit("vector", lambda e, ii=i: e.scalar_tensor_tensor(self.tq[:], xt(ii), negm(ii),
                                                                       cb_rep, ALU.add, ALU.mult),
                      waits=[("V", max(v_sb, v_cb))], inc="V")
            self.emit("vector", lambda e, ii=i: e.scalar_tensor_tensor(self.tm[:], self.tq[:], rstd(ii),
                                                                       sb_rep, ALU.mult, ALU.add), inc="V")
            self.emit("vector", lambda e: e.reduce_max(asc, self.tm[:], axis=AX,
                                                       apply_absolute_value=True), inc="V")
            v_asc = self.semv["V"]
            self.emit("scalar", lambda e: e.activation(self.smalls[:, 16:17], asc, AF.Copy, bias=EPS_FFN),
                      waits=[("V", v_asc), ("S", self.s_wsbc)], inc="S")
            self.emit("scalar", lambda e, ii=i: e.activation(ffns(ii), self.smalls[:, 16:17], AF.Copy,
                                                             scale=self.smalls[:, 8:9]), inc="S")
            s_asc = self.semv["S"]
            self.emit("vector", lambda e: e.reciprocal(self.smalls[:, 4:5], self.smalls[:, 16:17]),
                      waits=[("S", s_asc)], inc="V")
            v_rec = self.semv["V"]
            self.emit("scalar", lambda e: e.activation(self.smalls[:, 20:21], self.smalls[:, 4:5],
                                                       AF.Copy, scale=127.0),
                      waits=[("V", v_rec)], inc="S")
            s_sinv = self.semv["S"]
            ztile = self.h1b[:, D * slot:D * (slot + 1)]
            w0 = [(f"TRZ{slot}", 16 * trz[slot])] if i >= 2 else []
            w0.append(("S", s_sinv))
            self.emit("vector", lambda e: e.tensor_scalar(self.tq[:], self.tm[:], self.smalls[:, 20:21],
                                                          128.0, ALU.mult, ALU.add), waits=w0, inc="V")
            self.emit("vector", lambda e: e.tensor_scalar(self.tq[:], self.tq[:], 255.0, 0.0,
                                                          ALU.min, ALU.max), inc="V")
            self.emit("vector", lambda e: e.tensor_copy(self.zi[:], self.tq[:]), inc="V")
            self.emit("vector", lambda e: e.scalar_tensor_tensor(self.tm[:], self.tq[:], -0.5, self.zi[:],
                                                                 ALU.add, ALU.is_ge), inc="V")
            self.emit("vector", lambda e, z_=ztile: e.scalar_tensor_tensor(z_, self.zi[:], -128.0, self.tm[:],
                                                                           ALU.add, ALU.add), inc="V")
            v_z = self.semv["V"]
            self.dma("scalar", zT_view[:, :, 128 * i:128 * (i + 1)], ztile,
                     waits=[("V", v_z)], inc=f"TRZ{slot}", transpose=True)
            trz[slot] += 1
            zinfo[i] = (slot, trz[slot])

        v_gb, _ = self.replicate_mod(5, 0, clamp=True)
        gb_rep = self.rep[:, 0:1024]
        for ci in range(NCH):
            self.emit("vector", lambda e, cc=ci: e.tensor_mul(
                self.bigA[:, 1024 * cc:1024 * (cc + 1)], self.bigA[:, 1024 * cc:1024 * (cc + 1)],
                gb_rep),
                waits=[("V", v_gb), ("TRW", 16 * self.trw_total)] if ci == 0 else [], inc="V")
        v_gbfold = self.semv["V"]

        evac_done = dict(self.psm_last)
        for i in range(NT):
            half = i % 2
            slot, k = zinfo[i]
            for ci in range(NCH):
                for p in range(2):
                    w0 = []
                    if ci == 0 and p == 0:
                        w0 = [(f"TRZ{slot}", 16 * k), ("TRW", 16 * self.trw_total),
                              ("V", max(v_gbfold, evac_done[half]))]
                    self.mm(self.psM[:, 1024 * half + 512 * p:1024 * half + 512 * (p + 1)],
                            self.bigB[:, T * ci + 128 * i:T * ci + 128 * (i + 1)],
                            self.bigA[:, 1024 * ci + 512 * p:1024 * ci + 512 * (p + 1)],
                            start=(ci == 0), stop=(ci == NCH - 1), waits=w0)
            t_f = self.semv["T"]
            self.emit("vector",
                      lambda e, ii=i, hh=half: e.scalar_tensor_tensor(
                          xt(ii), self.psM[:, 1024 * hh:1024 * (hh + 1)],
                          ffns(ii), xt(ii), ALU.mult, ALU.add),
                      waits=[("T", t_f)], inc="V")
            evac_done[half] = self.semv["V"]
            v_out = self.semv["V"]
            self.dma("sync", self.h_out[128 * i:128 * (i + 1), :], xt(i),
                     waits=[("V", v_out)], inc="OUT")

    def finish(self):
        if self.stop is None:
            self.emit("sync", lambda e: e.nop(), waits=[("OUT", 16 * NT)])
        else:
            allw = [(s_, v) for s_, v in self.semv.items() if v > 0 and s_ != "OUT"]
            self.emit("sync", lambda e: e.nop(), waits=allw)
            for i in range(NT):
                self.dma("sync", self.h_out[128 * i:128 * (i + 1), :],
                         self.x_sb[:, D * i:D * (i + 1)], inc="OUT")
        if self.debug:
            for name, src in [("mod", self.mod_row), ("bigA", self.bigA),
                              ("bigB", self.bigB), ("ckvT", self.ckvT),
                              ("aux1", self.aux1), ("wmodb", self.wmodb),
                              ("stats", self.stats), ("smalls", self.smalls)]:
                self.dma("sync", self.h_dbg[name][:, :], src[:, :],
                         waits=[("OUT", self.semv["OUT"])], inc="OUT")
        self.emit("sync", lambda e: e.nop(), waits=[("OUT", self.semv["OUT"])])

    def finalize(self):
        nc = self.nc
        with nc.Block() as block:
            @block.sync
            def _(e):
                for f in self.ops["sync"]:
                    f(e)

            @block.gpsimd
            def _(e):
                for f in self.ops["gpsimd"]:
                    f(e)

            @block.tensor
            def _(e):
                for f in self.ops["tensor"]:
                    f(e)

            @block.vector
            def _(e):
                for f in self.ops["vector"]:
                    f(e)

            @block.scalar
            def _(e):
                for f in self.ops["scalar"]:
                    f(e)
        return nc


def make_consts():
    inv = 1.0 / 10000.0 ** (np.arange(0, DR, 2, dtype=np.float64) / DR)
    t = np.arange(T, dtype=np.float64)
    fr = np.outer(t, inv)
    cossin = np.concatenate([np.cos(fr).T, np.sin(fr).T], axis=0).astype(NPBF)
    tri = np.triu(np.ones((128, 128), np.float32)).astype(NPBF)
    pairsel = np.zeros((16, 1024), np.float32)
    for c in range(8):
        pairsel[2 * c, 128 * c:128 * c + 64] = 1.0
        pairsel[2 * c + 1, 128 * c + 64:128 * c + 128] = 1.0
    return cossin, tri, pairsel.astype(NPBF)


_CACHE = {}


def kernel(L0_attn_x, L0_attn_W_kv_down, L0_attn_W_k_up, L0_attn_W_v_up, L0_attn_W_q,
           L0_attn_W_q_rope, L0_attn_W_k_rope, L0_attn_W_o, L0_ffn_weight, L0_c, L0_wmod):
    if "nc" not in _CACHE:
        _CACHE["nc"] = Builder(debug=False).finalize()
    nc = _CACHE["nc"]
    cossin, tri, pairsel = make_consts()
    x = np.ascontiguousarray(np.asarray(L0_attn_x, dtype=np.float32))
    c = np.asarray(L0_c, dtype=np.float32)
    common = {
        "w_kv_down": np.ascontiguousarray(L0_attn_W_kv_down, dtype=np.float32),
        "w_k_up": np.ascontiguousarray(L0_attn_W_k_up, dtype=np.float32),
        "w_v_up": np.ascontiguousarray(L0_attn_W_v_up, dtype=np.float32),
        "w_q": np.ascontiguousarray(L0_attn_W_q, dtype=np.float32),
        "w_q_rope": np.ascontiguousarray(L0_attn_W_q_rope, dtype=np.float32),
        "w_k_rope": np.ascontiguousarray(L0_attn_W_k_rope, dtype=np.float32),
        "w_o": np.ascontiguousarray(L0_attn_W_o, dtype=np.float32),
        "w_ffn": np.ascontiguousarray(L0_ffn_weight, dtype=np.float32),
        "w_mod": np.ascontiguousarray(L0_wmod, dtype=np.float32),
        "cossin": cossin,
        "trimask": tri,
        "pairsel": pairsel,
    }
    in_maps = []
    for bidx in range(B):
        m = dict(common)
        m["x"] = np.ascontiguousarray(x[bidx])
        m["cT"] = np.ascontiguousarray(c[bidx].reshape(8, 128).T)
        in_maps.append(m)
    res = run_bass_kernel_spmd(nc, in_maps, core_ids=list(range(B)))
    return np.stack([r["out"] for r in res.results], axis=0)


if __name__ == "__main__":
    b = Builder(debug=False)
    nc = b.finalize()
    print("build ok; instructions:", {k: len(v) for k, v in b.ops.items()})


# version 07 (final)
# speedup vs baseline: 1.0271x; 1.0033x over previous
import numpy as np
import ml_dtypes
from contextlib import ExitStack

import concourse.bass as bass
import concourse.mybir as mybir
from concourse.bass_utils import run_bass_kernel_spmd

F32 = mybir.dt.float32
BF16 = mybir.dt.bfloat16
I32 = mybir.dt.int32
AF = mybir.ActivationFunctionType
ALU = mybir.AluOpType
AX = mybir.AxisListType.X

B, T, D, DC, DR, NH, HD = 8, 2048, 1024, 128, 32, 16, 64
NT = T // 128
NCH = D // 128
SCALE = float((HD + DR) ** -0.5)
SHIFT = 12.0
EPS_LN = 1e-5
EPS_FFN = 1e-8

NPBF = ml_dtypes.bfloat16


class Builder:
    ENGINES = ("sync", "gpsimd", "tensor", "vector", "scalar")

    def __init__(self, debug=False, stop=None):
        self.debug = debug
        self.stop = stop
        self.nc = bass.Bass("TRN2", target_bir_lowering=False)
        self.ops = {e: [] for e in self.ENGINES}
        self.semv = {}
        self.sems = {}
        self.ctx = ExitStack()
        self.lastw = {}
        self.build()

    def sem(self, name):
        self.sems[name] = self.ctx.enter_context(self.nc.semaphore(name))
        self.semv[name] = 0

    def w(self, name):
        return (name, self.semv[name])

    def emit(self, eng, fn, waits=(), inc=None, incn=1):
        waits = [(s, v) for (s, v) in waits if v > 0]
        for s, v in waits:
            assert s == "MODEV" or v <= self.semv[s], f"wait on {s}={v} > planned {self.semv[s]}"
        if inc is not None:
            self.semv[inc] += incn
        sems, lastw = self.sems, self.lastw

        def run(e, _fn=fn, _w=waits, _i=inc, _e=eng):
            for s, v in _w:
                k = (_e, s)
                if lastw.get(k, -1) < v:
                    e.wait_ge(sems[s], v)
                    lastw[k] = v
            r = _fn(e)
            if _i is not None:
                r.then_inc(sems[_i], incn)

        self.ops[eng].append(run)

    def dma(self, eng, out, in_, waits=(), inc=None, transpose=False):
        if transpose:
            self.emit(eng, lambda e, o=out, i=in_: e.dma_start_transpose(o, i),
                      waits=waits, inc=inc, incn=16)
        else:
            self.emit(eng, lambda e, o=out, i=in_: e.dma_start(o, i),
                      waits=waits, inc=inc, incn=16)

    def mm(self, out, lhsT, rhs, start=True, stop=True, waits=(), tp=None):
        self.emit("tensor",
                  lambda e, o=out, l=lhsT, r=rhs, s1=start, s2=stop, t=tp:
                  e.matmul(o, l, r, start=s1, stop=s2, tile_position=t),
                  waits=waits, inc="T")
        return self.semv["T"]

    def build(self):
        nc = self.nc
        E = self.ctx.enter_context
        ap = lambda t: t.ap()

        self.h_x = ap(nc.dram_tensor("x", [T, D], F32, kind="ExternalInput"))
        self.h_wkv = ap(nc.dram_tensor("w_kv_down", [D, DC], F32, kind="ExternalInput"))
        self.h_wku = ap(nc.dram_tensor("w_k_up", [DC, D], F32, kind="ExternalInput"))
        self.h_wvu = ap(nc.dram_tensor("w_v_up", [DC, D], F32, kind="ExternalInput"))
        self.h_wq = ap(nc.dram_tensor("w_q", [D, D], F32, kind="ExternalInput"))
        self.h_wqr = ap(nc.dram_tensor("w_q_rope", [D, DR], F32, kind="ExternalInput"))
        self.h_wkr = ap(nc.dram_tensor("w_k_rope", [DC, DR], F32, kind="ExternalInput"))
        self.h_wo = ap(nc.dram_tensor("w_o", [D, D], F32, kind="ExternalInput"))
        self.h_wf = ap(nc.dram_tensor("w_ffn", [D, D], F32, kind="ExternalInput"))
        self.h_wmod = ap(nc.dram_tensor("w_mod", [D, 6 * D], F32, kind="ExternalInput"))
        self.h_cT = ap(nc.dram_tensor("cT", [128, 8], F32, kind="ExternalInput"))
        self.h_cossin = ap(nc.dram_tensor("cossin", [32, T], BF16, kind="ExternalInput"))
        self.h_trimask = ap(nc.dram_tensor("trimask", [128, 128], BF16, kind="ExternalInput"))
        self.h_pairsel = ap(nc.dram_tensor("pairsel", [16, 1024], BF16, kind="ExternalInput"))
        self.h_out = ap(nc.dram_tensor("out", [T, D], F32, kind="ExternalOutput"))
        if self.debug:
            self.h_dbg = {}
            for name, shape, dt in [("mod", [1, 6 * D], BF16), ("bigA", [128, 16384], BF16),
                                    ("bigB", [128, 16384], BF16), ("ckvT", [128, T], BF16),
                                    ("aux1", [128, T], BF16), ("wmodb", [128, 2048], BF16),
                                    ("stats", [128, 320], F32), ("smalls", [128, 32], F32)]:
                self.h_dbg[name] = ap(nc.dram_tensor("dbg_" + name, shape, dt,
                                                     kind="ExternalOutput"))

        sb = lambda name, shape, dt: E(nc.sbuf_tensor(name, shape, dt))
        self.x_sb = sb("x_sb", [128, NT * D], F32)
        self.bigA = sb("bigA", [128, 16384], BF16)
        self.bigB = sb("bigB", [128, 16384], BF16)
        self.mod_row = sb("mod_row", [1, 6 * D], BF16)
        self.ckvT = sb("ckvT", [128, T], BF16)
        self.aux1 = sb("aux1", [128, T], BF16)
        self.cossin = sb("cossin_sb", [128, 2 * T], BF16)
        self.rep = sb("rep", [128, 2 * D], BF16)
        self.wmodb = sb("wmodb", [128, 2048], BF16)
        self.wqwo = sb("wqwo", [128, NCH * D], BF16)
        self.wkv = sb("wkv", [128, NCH * DC], BF16)
        self.wqr = sb("wqr", [128, NCH * DR], BF16)
        self.wku = sb("wku", [128, D], BF16)
        self.wvu = sb("wvu", [128, D], BF16)
        self.wkr = sb("wkr", [128, DR], BF16)
        self.tq = sb("tq", [128, D], F32)
        self.tm = sb("tm", [128, D], F32)
        self.h1b = sb("h1b", [128, 2 * D], BF16)
        self.zi = sb("zi", [128, D], I32)
        self.cpool = sb("cpool", [128, 12], F32)
        self.cT = sb("cT_sb", [128, 8], F32)
        self.cTb = sb("cTb", [128, 8], BF16)
        self.trimask = sb("trimask_sb", [128, 128], BF16)
        self.pairsel = sb("pairsel_sb", [16, 1024], BF16)
        self.ones1 = sb("ones1", [1, 128], BF16)
        self.ones1f = sb("ones1f", [1, 128], F32)
        self.ones128f = sb("ones128f", [128, 1], F32)
        self.stats = sb("stats", [128, 320], F32)
        self.smalls = sb("smalls", [128, 32], F32)
        self.wsrow = sb("wsrow", [1, 8], F32)

        self.consts = [-SHIFT, SCALE, 1.0, 1.0 / D, EPS_LN, 1.0 / (D * D), EPS_FFN,
                       -1.0 / D, 1.0 / 127.0, 127.0, 128.0]
        for i, v in enumerate(self.consts):
            nc.const_aps.register(F32, float(v), self.cpool[:, i:i + 1])

        self.psA = E(nc.psum_tensor("psA", [128, 1024], F32))
        self.psB = E(nc.psum_tensor("psB", [128, 1024], F32))
        self.psM = E(nc.psum_tensor("psM", [128, 2048], F32))

        for s in ("T", "V", "S", "W", "WO", "CONST", "MODA", "MODB",
                  "SX0", "SX1", "SX2", "SX3", "TRH0", "TRH1", "TRZ0", "TRZ1",
                  "TRW", "WFA", "WFB", "OUT", "QPK", "OAT", "RCP0", "RCP1", "RP", "MODEV"):
            self.sem(s)

        order = ["p0", "p1", "p2", "p3", "p4", "p5"]
        fns = [self.phase0, self.phase1, self.phase2, self.phase3, self.phase4, self.phase5]
        for nm, fn in zip(order, fns):
            fn()
            if self.stop == nm:
                break
        self.finish()

    def phase0(self):
        def init(e):
            for i, v in enumerate(self.consts):
                e.memset(self.cpool[:, i:i + 1], float(v))
            e.memset(self.ones1[:], 1.0)
            e.memset(self.ones1f[:], 1.0)
            return e.memset(self.ones128f[:], 1.0)
        self.emit("vector", init, inc="V")
        self.v_init = self.semv["V"]

        self.dma("sync", self.cT[:], self.h_cT[:, :], inc="CONST")
        for q in range(4):
            self.dma("sync", self.x_sb.ap().rearrange("p (i d) -> p i d", d=D)[:, 4 * q:4 * q + 4, :],
                     self.h_x.rearrange("(i p) d -> p i d", p=128)[:, 4 * q:4 * q + 4, :],
                     inc=f"SX{q}")
        self.dma("sync", self.cossin[64:80, 0:T], self.h_cossin[0:16, :], inc="CONST")
        self.dma("sync", self.cossin[64:80, T:2 * T], self.h_cossin[16:32, :], inc="CONST")
        self.dma("sync", self.trimask[:], self.h_trimask[:, :], inc="CONST")
        self.dma("sync", self.pairsel[:], self.h_pairsel[:, :], inc="CONST")
        self.const_total = self.semv["CONST"]

        def emit_weight_loads():
            self.dma("gpsimd", self.wqwo.ap().rearrange("p (j o) -> p j o", o=D),
                     self.h_wq.rearrange("(j p) o -> p j o", p=128), inc="W")
            self.dma("gpsimd", self.wkv.ap().rearrange("p (j o) -> p j o", o=DC),
                     self.h_wkv.rearrange("(j p) o -> p j o", p=128), inc="W")
            self.dma("gpsimd", self.wqr.ap().rearrange("p (j o) -> p j o", o=DR),
                     self.h_wqr.rearrange("(j p) o -> p j o", p=128), inc="W")
            self.dma("gpsimd", self.wku[:], self.h_wku[:, :], inc="W")
            self.dma("gpsimd", self.wvu[:], self.h_wvu[:, :], inc="W")
            self.dma("gpsimd", self.wkr[:], self.h_wkr[:, :], inc="W")
            self.w_all = self.w("W")

        self.emit("scalar", lambda e: e.activation(self.cT[:], self.cT[:], AF.Silu),
                  waits=[("CONST", self.const_total)], inc="S")
        self.emit("scalar", lambda e: e.copy(self.cTb[:], self.cT[:]), inc="S")
        s_ctb = self.semv["S"]

        slots = [self.wmodb[:, 0:1024], self.wmodb[:, 1024:2048]]
        names = ["MODA", "MODB"]
        self.pending_mod_evacs = []
        mm_done = {}
        blk = 0
        for g in range(6):
            half = (g % 2) * 1024
            for j in range(8):
                slot = blk % 2
                dwaits = [("T", mm_done[blk - 2])] if blk >= 2 else []
                self.dma("gpsimd", slots[slot],
                         self.h_wmod[128 * j:128 * (j + 1), 1024 * g:1024 * (g + 1)],
                         waits=dwaits, inc=names[slot])
                mwaits = [(names[slot], 16 * (blk // 2 + 1)), ("S", s_ctb)]
                if g >= 2 and j == 0:
                    mwaits.append(("MODEV", g - 1))
                for p in range(2):
                    self.mm(self.psM[0:1, half + 512 * p: half + 512 * (p + 1)],
                            self.cTb[:, j:j + 1], slots[slot][:, 512 * p:512 * (p + 1)],
                            start=(j == 0), stop=(j == 7),
                            waits=mwaits if p == 0 else [])
                mm_done[blk] = self.semv["T"]
                blk += 1
                if blk == 8:
                    emit_weight_loads()
            tg = self.semv["T"]
            if g < 2:
                self._emit_one_mod_evac(g, half, tg)
            else:
                self.pending_mod_evacs.append((g, half, tg))
        self.mod_stream_done = [("MODA", 16 * 24), ("MODB", 16 * 24)]

    def _emit_one_mod_evac(self, g, half, tg):
        dst = self.mod_row[0:1, 1024 * g:1024 * (g + 1)]
        if g in (1, 4):
            self.emit("scalar", lambda e, d=dst, hh=half: e.activation(
                d, self.psM[0:1, hh:hh + 1024], AF.Copy, bias=1.0),
                waits=[("T", tg)], inc="MODEV")
        else:
            self.emit("scalar", lambda e, d=dst, hh=half: e.copy(
                d, self.psM[0:1, hh:hh + 1024]),
                waits=[("T", tg)], inc="MODEV")

    def emit_mod_evacs(self, upto):
        while self.pending_mod_evacs and self.pending_mod_evacs[0][0] < upto:
            g, half, tg = self.pending_mod_evacs.pop(0)
            self._emit_one_mod_evac(g, half, tg)

    def replicate_mod(self, vec_idx, slot, clamp=False):
        ps = self.psA
        mmw = [("MODEV", vec_idx + 1), self.w("V"), self.w("S")]
        for p in range(2):
            self.mm(ps[:, 512 * p:512 * (p + 1)], self.ones1[:],
                    self.mod_row[0:1, 1024 * vec_idx + 512 * p:1024 * vec_idx + 512 * (p + 1)],
                    waits=mmw if p == 0 else [])
        t_done = self.semv["T"]
        dst = self.rep[:, 1024 * slot:1024 * (slot + 1)]
        if not clamp:
            self.emit("vector", lambda e: e.tensor_copy(dst, ps[:, 0:1024]),
                      waits=[("T", t_done)], inc="V")
        else:
            tq, tm = self.tq, self.tm
            self.emit("vector", lambda e: e.tensor_scalar(tq[:], ps[:, 0:1024], 0.0, None, ALU.is_ge),
                      waits=[("T", t_done)], inc="V")
            self.emit("vector", lambda e: e.tensor_scalar(tm[:], ps[:, 0:1024], 0.25, None, ALU.max), inc="V")
            self.emit("vector", lambda e: e.tensor_scalar(dst, ps[:, 0:1024], -0.25, None, ALU.min), inc="V")
            self.emit("vector", lambda e: e.tensor_sub(tm[:], tm[:], dst), inc="V")
            self.emit("vector", lambda e: e.tensor_mul(tm[:], tm[:], tq[:]), inc="V")
            self.emit("vector", lambda e: e.tensor_add(dst, dst, tm[:]), inc="V")
        return self.semv["V"], t_done

    def phase1(self):
        sqv = lambda i: self.stats[:, 256 + 4 * i:256 + 4 * i + 1]
        negm = lambda i: self.stats[:, 4 * i:4 * i + 1]
        varc = lambda i: self.stats[:, 64 + 4 * i:64 + 4 * i + 1]
        rstd = lambda i: self.stats[:, 128 + 4 * i:128 + 4 * i + 1]
        xt = lambda i: self.x_sb[:, D * i:D * (i + 1)]

        self.sev_rstd = {}
        var_v = {}
        for i in range(NT):
            self.emit("vector", lambda e, ii=i: e.reduce_sum(varc(ii), xt(ii), axis=AX),
                      waits=[(f"SX{i // 4}", 16), ("V", self.v_init)], inc="V")
            vm = self.semv["V"]
            self.emit("scalar", lambda e, ii=i: e.activation(negm(ii), varc(ii), AF.Copy,
                                                             scale=-1.0 / D),
                      waits=[("V", vm)], inc="S")
            sn = self.semv["S"]
            self.emit("vector", lambda e, ii=i: e.tensor_scalar(self.tq[:], xt(ii), negm(ii), None,
                                                                ALU.add),
                      waits=[("S", sn)], inc="V")
            self.emit("vector", lambda e: e.tensor_mul(self.tm[:], self.tq[:], self.tq[:]), inc="V")
            self.emit("vector", lambda e, ii=i: e.reduce_sum(varc(ii), self.tm[:], axis=AX), inc="V")
            self.last_var_v = self.semv["V"]
            var_v[i] = self.last_var_v
            for ii2 in ([i - 1] if i >= 1 else []) + ([i] if i == NT - 1 else []):
                self.emit("scalar", lambda e, jj=ii2: e.activation(sqv(jj), varc(jj), AF.Sqrt,
                                                                   bias=EPS_LN, scale=1.0 / D),
                          waits=[("V", var_v[ii2])], inc="S")
                sv = self.semv["S"]
                self.emit("vector", lambda e, jj=ii2: e.reciprocal(rstd(jj), sqv(jj)),
                          waits=[("S", sv)], inc="V")
                self.sev_rstd[ii2] = self.semv["V"]

        self.emit_mod_evacs(6)
        v_sa, _ = self.replicate_mod(0, 0)
        v_ca, self.t_carep = self.replicate_mod(1, 1)
        sa_rep = self.rep[:, 0:1024]
        ca_rep = self.rep[:, 1024:2048]

        h1T_dst = self.bigA.ap().rearrange("p (j t) -> p j t", t=T)
        trh = [0, 0]
        for i in range(NT):
            slot = i % 2
            h1tile = self.h1b[:, D * slot:D * (slot + 1)]
            waits = [("V", max(v_sa, v_ca, self.sev_rstd[i]))]
            if i >= 2:
                waits.append((f"TRH{slot}", 16 * trh[slot]))
            self.emit("vector",
                      lambda e, ii=i: e.scalar_tensor_tensor(self.tm[:], xt(ii), negm(ii), ca_rep,
                                                             ALU.add, ALU.mult),
                      waits=waits, inc="V")
            self.emit("vector",
                      lambda e, ii=i, h1=h1tile: e.scalar_tensor_tensor(h1, self.tm[:], rstd(ii), sa_rep,
                                                                        ALU.mult, ALU.add),
                      inc="V")
            vap = self.semv["V"]
            self.dma("scalar", h1T_dst[:, :, 128 * i:128 * (i + 1)], h1tile,
                     waits=[("V", vap)], inc=f"TRH{slot}", transpose=True)
            trh[slot] += 1
        self.p1_done_w = [("TRH0", 16 * trh[0]), ("TRH1", 16 * trh[1])]

    def phase2(self):
        h1T = lambda j: self.bigA[:, T * j:T * (j + 1)]

        for p in range(4):
            for j in range(NCH):
                w0 = []
                if p == 0 and j == 0:
                    w0 = self.p1_done_w + [self.w_all, ("MODEV", 6)]
                self.mm(self.psM[:, 512 * p:512 * (p + 1)],
                        self.wkv[:, DC * j:DC * (j + 1)],
                        h1T(j)[:, 512 * p:512 * (p + 1)],
                        start=(j == 0), stop=(j == NCH - 1), waits=w0)
        t_ckv = self.semv["T"]
        self.emit("scalar", lambda e: e.copy(self.ckvT[:], self.psM[:, 0:2048]),
                  waits=[("T", t_ckv)], inc="S")
        s_ckv = self.semv["S"]

        def rope(dst, stat_x1, stat_x2, nacc, mov_of, extra_first=(), extra_v=()):
            cosv = self.cossin[64:80, :]
            first = list(extra_first)
            v_last = 0
            for q in range(4):
                lo = 512 * q
                ps = self.psA if q % 2 == 0 else self.psB
                for j in range(nacc):
                    self.mm(ps[64:80, 0:512], stat_x1(j), mov_of(j, lo, 512),
                            start=(j == 0), stop=(j == nacc - 1),
                            waits=(first + ([("V", v_last)] if q >= 2 else [])) if j == 0 else [],
                            tp=(0, 64))
                    first = []
                    self.mm(ps[64:80, 512:1024], stat_x2(j), mov_of(j, lo, 512),
                            start=(j == 0), stop=(j == nacc - 1), tp=(0, 64))
                t_mm = self.semv["T"]
                x1 = ps[64:80, 0:512]
                x2 = ps[64:80, 512:1024]
                cosp = cosv[:, lo:lo + 512]
                sinp = cosv[:, T + lo:T + lo + 512]
                th = self.tq[64:80, 0:512]
                uh = self.tq[64:80, 512:1024]
                self.emit("vector", lambda e, a=x1, b=cosp, c=th: e.tensor_mul(c, a, b),
                          waits=[("T", t_mm), ("CONST", self.const_total)] + list(extra_v), inc="V")
                extra_v = ()
                self.emit("vector", lambda e, a=x2, b=sinp, c=uh: e.tensor_mul(c, a, b), inc="V")
                self.emit("vector", lambda e, a=th, b=uh, c=dst[64:80, lo:lo + 512]:
                          e.tensor_sub(c, a, b), inc="V")
                self.emit("vector", lambda e, a=x2, b=cosp, c=th: e.tensor_mul(c, a, b), inc="V")
                self.emit("vector", lambda e, a=x1, b=sinp, c=uh: e.tensor_mul(c, a, b), inc="V")
                self.emit("vector", lambda e, a=th, b=uh, c=self.h1b[64:80, lo:lo + 512]:
                          e.tensor_add(c, a, b), inc="V")
                v_last = self.semv["V"]
            self.dma("gpsimd", dst[96:112, 0:T], self.h1b[64:80, 0:T],
                     waits=[("V", v_last)] + self.p1_done_w, inc="RP")
            return v_last

        rope(self.wmodb,
             lambda j: self.wkr[:, 0:16], lambda j: self.wkr[:, 16:32], 1,
             lambda j, lo, w: self.ckvT[:, lo:lo + w],
             extra_first=[("S", s_ckv), self.w("V")] + self.mod_stream_done)

        s_prev_evac = s_ckv
        for m in range(NCH):
            for j in range(NCH):
                for p in range(4):
                    w0 = []
                    if j == 0 and p == 0:
                        w0 = [("S", s_prev_evac)]
                    self.mm(self.psM[:, 512 * p:512 * (p + 1)],
                            self.wqwo[:, D * j + 128 * m:D * j + 128 * (m + 1)],
                            h1T(j)[:, 512 * p:512 * (p + 1)],
                            start=(j == 0), stop=(j == NCH - 1), waits=w0)
            t_q = self.semv["T"]
            self.emit("scalar", lambda e, mm_=m: e.copy(self.bigB[:, T * mm_:T * (mm_ + 1)],
                                                        self.psM[:, 0:2048]),
                      waits=[("T", t_q)], inc="S")
            s_prev_evac = self.semv["S"]
        self.sev_qT = self.semv["S"]

        v_qr = rope(self.aux1,
                    lambda j: self.wqr[:, DR * j:DR * j + 16],
                    lambda j: self.wqr[:, DR * j + 16:DR * j + 32], NCH,
                    lambda j, lo, w: h1T(j)[:, lo:lo + w], extra_v=[("RP", 16)])
        self.sev_rope = v_qr
        self.t_h1T_done = self.semv["T"]
        self.rp_total = self.semv["RP"]

    def phase3(self):
        KH = [self.bigA[:, 0:T], self.bigA[:, T:2 * T]]
        QH = [self.bigA[:, 2 * T:3 * T], self.bigA[:, 3 * T:4 * T]]
        VH = [self.bigA[:, 8192:8192 + 1040], self.bigA[:, 9232:9232 + 1040]]
        STG = [self.bigA[:, 10272 + 1024 * s:10272 + 1024 * (s + 1)] for s in range(4)]
        chunk = lambda c: self.bigB[:, T * c:T * (c + 1)]

        w_hd = [("T", self.t_h1T_done), ("RP", self.rp_total)]
        for pi in range(2):
            self.emit("vector", lambda e, b=KH[pi]: e.memset(b[64:128, :], 0.0),
                      waits=w_hd, inc="V")
            w_hd = []
            self.emit("vector", lambda e, b=QH[pi]: e.memset(b[64:128, :], 0.0), inc="V")
            self.emit("vector", lambda e, b=KH[pi]: e.tensor_copy(b[64:80, :], self.wmodb[64:80, :]),
                      inc="V")
            self.emit("vector", lambda e, b=KH[pi]: e.tensor_copy(b[96:112, :], self.wmodb[96:112, :]),
                      inc="V")
            self.emit("vector", lambda e, b=QH[pi]: e.tensor_copy(b[64:80, :], self.aux1[64:80, :]),
                      inc="V")
            self.emit("vector", lambda e, b=QH[pi]: e.tensor_copy(b[96:112, :], self.aux1[96:112, :]),
                      inc="V")
        self.emit("vector", lambda e: e.memset(VH[0], 1.0), inc="V")
        self.emit("vector", lambda e: e.memset(VH[1], 1.0), inc="V")
        v_ones = self.semv["V"]

        exp_done_ps = {}
        self.recip_rows = 0
        qpk = [0]

        scores_done_prev = 0
        av_done_prev = 0
        v_prep = v_ones
        v_half1_evac_prev = 0

        def prep_stageA(h, sc_prev, psm_ok):
            pi = h % 2
            for p in range(2):
                w0 = list(psm_ok) if p == 0 else []
                self.mm(self.psM[0:64, 512 * p:512 * (p + 1)],
                        self.wku[:, HD * h:HD * (h + 1)],
                        self.ckvT[:, 512 * p:512 * (p + 1)], waits=w0)
            t1 = self.semv["T"]
            self.emit("vector", lambda e, pi_=pi: e.tensor_copy(KH[pi_][0:64, 0:1024],
                                                                self.psM[0:64, 0:1024]),
                      waits=[("T", max(t1, sc_prev))], inc="V")
            return self.semv["V"]

        def prep_stageB(h, v1):
            pi = h % 2
            for p in range(2):
                w0 = [("V", v1)] if p == 0 else []
                self.mm(self.psM[0:64, 512 * p:512 * (p + 1)],
                        self.wku[:, HD * h:HD * (h + 1)],
                        self.ckvT[:, 1024 + 512 * p:1024 + 512 * (p + 1)], waits=w0)
            t2 = self.semv["T"]
            self.emit("vector", lambda e, pi_=pi: e.tensor_copy(KH[pi_][0:64, 1024:2048],
                                                                self.psM[0:64, 0:1024]),
                      waits=[("T", t2)], inc="V")
            return self.semv["V"]

        def prep_stageC(h, av_prev, v2):
            pi = h % 2
            for j in range(NT):
                w0 = [("V", v2)] if j == 0 else []
                self.mm(self.psM[:, 64 * j:64 * (j + 1)],
                        self.ckvT[:, 128 * j:128 * (j + 1)],
                        self.wvu[:, HD * h:HD * (h + 1)], waits=w0)
            t_vup = self.semv["T"]
            self.emit("vector",
                      lambda e, pi_=pi: e.tensor_copy(
                          VH[pi_].rearrange("p (j g) -> p j g", g=65)[:, :, 0:64],
                          self.psM.ap().rearrange("p (j g) -> p j g", g=64)[:, 0:16, :]),
                      waits=[("T", max(t_vup, av_prev)), ("V", v_ones)], inc="V")
            if h % 2 == 0:
                self.emit("vector", lambda e, hh=h, pi_=pi: e.tensor_copy(
                    QH[pi_][0:64, :], chunk(hh // 2)[0:64, :]),
                    waits=[("S", self.sev_qT)], inc="V")
            else:
                self.dma("sync", QH[pi][0:64, :], chunk(h // 2)[64:128, :],
                         waits=[("S", self.sev_qT)], inc="QPK")
                qpk[0] += 1
            return self.semv["V"], qpk[0]

        def emit_prep(h, sc_prev, av_prev, psab_free, psm_ok):
            v1 = prep_stageA(h, sc_prev, psm_ok)
            v2 = prep_stageB(h, v1)
            return prep_stageC(h, av_prev, v2)

        v_prep, q_prep = emit_prep(0, 0, 0, 0, [self.w("S"), self.w("V"), ("V", self.sev_rope)])

        for h in range(NH):
            pi = h % 2
            blkctr = 0
            binfo_of_j = {}
            first_w = [("V", v_prep)]
            if pi == 1:
                first_w.append(("QPK", 16 * q_prep))
            mid_done = {}

            def do_scores(j):
                nonlocal blkctr, first_w
                q0 = 128 * j
                binfo = []
                bstart = q0
                while bstart < T:
                    wblk = min(1024, T - bstart)
                    par = blkctr % 2
                    ps = self.psA if par == 0 else self.psB
                    stg = blkctr % 4
                    w0 = first_w
                    first_w = []
                    if par in exp_done_ps:
                        w0 = w0 + [("S", exp_done_ps[par])]
                    off = 0
                    while off < wblk:
                        pw = min(512, wblk - off)
                        self.mm(ps[:, off:off + pw],
                                KH[pi][:, 128 * j:128 * (j + 1)],
                                QH[pi][:, bstart + off:bstart + off + pw], waits=w0)
                        w0 = []
                        off += 512
                    t_mm = self.semv["T"]
                    self.emit("scalar",
                              lambda e, ps_=ps, w_=wblk, s_=stg: e.activation(
                                  STG[s_][:, 0:w_], ps_[:, 0:w_], AF.Exp,
                                  bias=-SHIFT, scale=SCALE),
                              waits=[("T", t_mm)], inc="S")
                    exp_done_ps[par] = self.semv["S"]
                    s_exp = self.semv["S"]
                    v_mask = 0
                    if bstart == q0:
                        self.emit("vector",
                                  lambda e, s_=stg: e.tensor_mul(
                                      STG[s_][:, 0:128], STG[s_][:, 0:128], self.trimask[:]),
                                  waits=[("S", s_exp)], inc="V")
                        v_mask = self.semv["V"]
                    binfo.append((bstart, wblk, stg, s_exp, v_mask))
                    blkctr += 1
                    bstart += wblk
                binfo_of_j[j] = binfo

            def do_av(j, half1_guard):
                for (bstart, wblk, stg, s_exp, v_mask) in binfo_of_j.pop(j):
                    s0 = bstart
                    while s0 < bstart + wblk:
                        s1 = min((s0 // 512 + 1) * 512, bstart + wblk)
                        gw = s1 - s0
                        reg_last = min(NT - 1, (s1 - 1) // 128)
                        w0 = [("S", s_exp)]
                        if v_mask:
                            w0.append(("V", v_mask))
                        if j == 0 and half1_guard:
                            w0.append(("V", half1_guard))
                        self.mm(self.psM[0:65, s0:s1],
                                VH[pi][:, 65 * j:65 * j + 65],
                                STG[stg][:, s0 - bstart:s0 - bstart + gw],
                                start=(j == 0), stop=(j == reg_last), waits=w0)
                        s0 = s1

            def evac_attn_half(h_, half, t_guard):
                pi_ = h_ % 2
                lo = 1024 * half
                dst_rows = (64 * (h_ % 2), 64 * (h_ % 2) + 64)
                if h_ % 2 == 0:
                    self.emit("vector", lambda e, hh=h_, l=lo: e.tensor_copy(
                        chunk(hh // 2)[0:64, l:l + 1024], self.psM[0:64, l:l + 1024]),
                        waits=[("T", t_guard)], inc="V")
                else:
                    self.emit("vector", lambda e, hh=h_, l=lo: e.tensor_copy(
                        self.h1b[0:64, l:l + 1024], self.psM[0:64, l:l + 1024]),
                        waits=[("T", t_guard), ("OAT", 16 * (2 * (h_ // 2) + half))], inc="V")
                    v_st = self.semv["V"]
                    self.dma("sync", chunk(h_ // 2)[64:128, lo:lo + 1024],
                             self.h1b[0:64, lo:lo + 1024],
                             waits=[("V", v_st)], inc="OAT")
                dslot = 64 + 32 * (h_ % 2)
                self.emit("vector", lambda e, l=lo, ds=dslot: e.tensor_copy(
                    self.h1b[ds:ds + 1, l:l + 1024], self.psM[64:65, l:l + 1024]),
                    waits=[("T", t_guard), (f"RCP{h_ % 2}", 16 * (h_ // 2))], inc="V")
                return self.semv["V"]

            pst = {}
            for j in range(NT):
                do_scores(j)
                if j >= 1:
                    do_av(j - 1, v_half1_evac_prev)
                if j == 8:
                    t_av7 = self.semv["T"]
                    pst["vh0"] = evac_attn_half(h, 0, t_av7)
                elif j == 10 and h + 1 < NH:
                    pst["v1"] = prep_stageA(h + 1, scores_done_prev, [("V", pst["vh0"])])
                elif j == 12 and h + 1 < NH:
                    pst["v2"] = prep_stageB(h + 1, pst["v1"])
                elif j == 14 and h + 1 < NH:
                    v_prep, q_prep = prep_stageC(h + 1, av_done_prev, pst["v2"])
            scores_done_cur = self.semv["T"]
            do_av(NT - 1, v_half1_evac_prev)
            t_av_all = self.semv["T"]
            v_h1 = evac_attn_half(h, 1, t_av_all)
            v_half1_evac_prev = v_h1
            dslot = 64 + 32 * (h % 2)
            self.dma("gpsimd", self.aux1[h:h + 1, :], self.h1b[dslot:dslot + 1, 0:2048],
                     waits=[("V", v_h1)], inc=f"RCP{h % 2}")
            scores_done_prev = scores_done_cur
            av_done_prev = t_av_all

        self.sev_attn_s = self.semv["S"]
        self.emit("vector", lambda e: e.reciprocal(self.tq[0:16, 0:1024], self.aux1[0:16, 0:1024]),
                  waits=[("RCP0", 16 * 8), ("RCP1", 16 * 8)], inc="V")
        self.emit("vector", lambda e: e.reciprocal(self.tm[0:16, 0:1024], self.aux1[0:16, 1024:2048]),
                  inc="V")
        self.emit("vector", lambda e: e.tensor_copy(self.aux1[0:16, 0:1024], self.tq[0:16, 0:1024]),
                  inc="V")
        self.emit("vector", lambda e: e.tensor_copy(self.aux1[0:16, 1024:2048], self.tm[0:16, 0:1024]),
                  inc="V")
        self.v_recip = self.semv["V"]

    def phase4(self):
        chunk = lambda c: self.bigB[:, T * c:T * (c + 1)]
        xt = lambda i: self.x_sb[:, D * i:D * (i + 1)]

        self.dma("gpsimd", self.wqwo.ap().rearrange("p (j o) -> p j o", o=D),
                 self.h_wo.rearrange("(j p) o -> p j o", p=128),
                 waits=[("S", self.sev_qT)], inc="WO")

        v_scaled = {-1: 0}
        for c in range(NCH):
            for p in range(4):
                w0 = []
                if p == 0:
                    w0 = [("V", max(self.v_recip, v_scaled[c - 1])), ("S", self.sev_attn_s),
                          ("OAT", 16 * 16)]
                self.mm(self.psM[:, 512 * p:512 * (p + 1)],
                        self.pairsel[:, 128 * c:128 * (c + 1)],
                        self.aux1[0:16, 512 * p:512 * (p + 1)], waits=w0)
            t_rep = self.semv["T"]
            self.emit("vector", lambda e, cc=c: e.tensor_mul(chunk(cc), chunk(cc), self.psM[:, :]),
                      waits=[("T", t_rep)], inc="V")
            v_scaled[c] = self.semv["V"]

        v_ga, _ = self.replicate_mod(2, 0, clamp=True)
        ga_rep = self.rep[:, 0:1024]
        for c in range(NCH):
            self.emit("vector", lambda e, cc=c: e.tensor_mul(
                self.wqwo[:, D * cc:D * (cc + 1)], self.wqwo[:, D * cc:D * (cc + 1)], ga_rep),
                waits=[self.w("WO"), ("V", v_ga)] if c == 0 else [], inc="V")
        v_wofold = self.semv["V"]

        v_x1 = {}
        evac_done = {0: 0, 1: 0}
        for i in range(NT):
            half = i % 2
            for c in range(NCH):
                for p in range(2):
                    w0 = []
                    if c == 0 and p == 0:
                        w0 = [("V", max(v_wofold, v_scaled[NCH - 1], evac_done[half]))]
                    self.mm(self.psM[:, 1024 * half + 512 * p:1024 * half + 512 * (p + 1)],
                            chunk(c)[:, 128 * i:128 * (i + 1)],
                            self.wqwo[:, D * c + 512 * p:D * c + 512 * (p + 1)],
                            start=(c == 0), stop=(c == NCH - 1), waits=w0)
            t_wo = self.semv["T"]
            self.emit("vector",
                      lambda e, ii=i, hh=half: e.scalar_tensor_tensor(
                          xt(ii), self.psM[:, 1024 * hh:1024 * (hh + 1)], 1.0, xt(ii),
                          ALU.mult, ALU.add),
                      waits=[("T", t_wo)], inc="V")
            evac_done[half] = self.semv["V"]
            v_x1[i] = self.semv["V"]
        self.v_x1 = v_x1
        self.psm_last = {0: v_x1[NT - 2], 1: v_x1[NT - 1]}

        wf_view = self.h_wf.rearrange("(c p) i -> c p i", p=128)
        slot_t = [self.tq, self.tm]
        names = ["WFA", "WFB"]
        cnt = [0, 0]
        red_done = {}
        v_gaclamp = self.semv["V"]
        for c in range(NCH):
            slot = c % 2
            w0 = [("V", v_gaclamp)] if c < 2 else [("S", red_done[c - 2])]
            self.dma("sync", slot_t[slot][:], wf_view[c], waits=w0, inc=names[slot])
            cnt[slot] += 1
            self.emit("scalar", lambda e, ss=slot, cc=c: e.activation(
                slot_t[ss][:], slot_t[ss][:], AF.Abs,
                accum_out=self.stats[:, 4 * cc:4 * cc + 1]),
                      waits=[(names[slot], 16 * cnt[slot])], inc="S")
            red_done[c] = self.semv["S"]
        s_cs = self.semv["S"]
        self.mm(self.psA[0:1, 0:8], self.ones128f[:],
                self.stats.ap().rearrange("p (a b) -> p a b", b=4)[:, 0:8, 0:1],
                waits=[("S", s_cs)])
        t_ws = self.semv["T"]
        self.emit("vector", lambda e: e.reduce_sum(self.wsrow[0:1, 0:1], self.psA[0:1, 0:8], axis=AX),
                  waits=[("T", t_ws)], inc="V")
        v_ws0 = self.semv["V"]
        self.emit("scalar", lambda e: e.activation(self.wsrow[0:1, 4:5], self.wsrow[0:1, 0:1],
                                                   AF.Copy, bias=EPS_FFN, scale=1.0 / (D * D)),
                  waits=[("V", v_ws0)], inc="S")
        s_ws = self.semv["S"]
        self.mm(self.psA[:, 8:9], self.ones1f[:], self.wsrow[0:1, 4:5], waits=[("S", s_ws)])
        t_bc = self.semv["T"]
        self.emit("vector", lambda e: e.tensor_copy(self.smalls[:, 0:1], self.psA[:, 8:9]),
                  waits=[("T", t_bc)], inc="V")
        self.emit("vector", lambda e: e.reciprocal(self.smalls[:, 4:5], self.psA[:, 8:9]),
                  inc="V")
        self.emit("scalar", lambda e: e.activation(self.smalls[:, 8:9], self.psA[:, 8:9],
                                                   AF.Copy, scale=1.0 / 127.0),
                  waits=[("T", t_bc)], inc="S")
        self.v_wsbc = self.semv["V"]
        self.s_wsbc = self.semv["S"]

        wqT_view = self.bigA.ap().rearrange("p (ci g) -> p ci g", g=1024)
        wq_slots = [self.ckvT[:, 0:1024], self.ckvT[:, 1024:2048]]
        rws = self.smalls[:, 4:5]
        red2 = {}
        trw = 0
        for c in range(NCH):
            slot = c % 2
            w0 = [("V", self.v_wsbc)] if c < 2 else [("V", red2[c - 2])]
            self.dma("sync", slot_t[slot][:], wf_view[c], waits=w0, inc=names[slot])
            cnt[slot] += 1
            tsrc = slot_t[slot]
            w1 = [(names[slot], 16 * cnt[slot])]
            if c >= 2:
                w1.append(("TRW", 16 * (c - 1)))
            self.emit("vector", lambda e, t_=tsrc: e.tensor_scalar(t_[:], t_[:], rws, 128.0,
                                                                   ALU.mult, ALU.add),
                      waits=w1, inc="V")
            self.emit("vector", lambda e, t_=tsrc: e.tensor_copy(self.zi[:], t_[:]), inc="V")
            self.emit("vector", lambda e, t_=tsrc: e.scalar_tensor_tensor(t_[:], t_[:], -0.5, self.zi[:],
                                                                          ALU.add, ALU.is_ge), inc="V")
            self.emit("vector", lambda e, t_=tsrc, ss=slot: e.scalar_tensor_tensor(
                wq_slots[ss], self.zi[:], -128.0, t_[:], ALU.add, ALU.add), inc="V")
            self.emit("vector", lambda e, ss=slot: e.tensor_scalar(wq_slots[ss], wq_slots[ss],
                                                                   1.0, -1.0, ALU.min, ALU.max), inc="V")
            red2[c] = self.semv["V"]
            self.dma("scalar", wqT_view[:, :, 128 * c:128 * (c + 1)], wq_slots[slot],
                     waits=[("V", red2[c])], inc="TRW", transpose=True)
            trw += 1
        self.trw_total = trw

    def phase5(self):
        negm = lambda i: self.stats[:, 4 * i:4 * i + 1]
        varc = lambda i: self.stats[:, 64 + 4 * i:64 + 4 * i + 1]
        rstd = lambda i: self.stats[:, 128 + 4 * i:128 + 4 * i + 1]
        ffns = lambda i: self.stats[:, 192 + 4 * i:192 + 4 * i + 1]
        sqv = lambda i: self.stats[:, 256 + 4 * i:256 + 4 * i + 1]
        xt = lambda i: self.x_sb[:, D * i:D * (i + 1)]
        zT_view = self.bigB.ap().rearrange("p (j t) -> p j t", t=T)
        asc = self.smalls[:, 12:13]

        v_sb, _ = self.replicate_mod(3, 0)
        v_cb, _ = self.replicate_mod(4, 1)
        sb_rep = self.rep[:, 0:1024]
        cb_rep = self.rep[:, 1024:2048]

        trz = [0, 0]
        zinfo = {}
        for i in range(NT):
            slot = i % 2
            self.emit("vector", lambda e, ii=i: e.reduce_sum(varc(ii), xt(ii), axis=AX),
                      waits=[("V", self.v_x1[i])], inc="V")
            vm = self.semv["V"]
            self.emit("scalar", lambda e, ii=i: e.activation(negm(ii), varc(ii), AF.Copy,
                                                             scale=-1.0 / D),
                      waits=[("V", vm)], inc="S")
            sn = self.semv["S"]
            self.emit("vector", lambda e, ii=i: e.tensor_scalar(self.tq[:], xt(ii), negm(ii), None,
                                                                ALU.add),
                      waits=[("S", sn), ("TRW", 16 * self.trw_total)], inc="V")
            self.emit("vector", lambda e: e.tensor_mul(self.tm[:], self.tq[:], self.tq[:]), inc="V")
            self.emit("vector", lambda e, ii=i: e.reduce_sum(varc(ii), self.tm[:], axis=AX), inc="V")
            vv = self.semv["V"]
            self.emit("scalar", lambda e, ii=i: e.activation(sqv(ii), varc(ii), AF.Sqrt,
                                                             bias=EPS_LN, scale=1.0 / D),
                      waits=[("V", vv)], inc="S")
            sv = self.semv["S"]
            self.emit("vector", lambda e, ii=i: e.reciprocal(rstd(ii), sqv(ii)),
                      waits=[("S", sv)], inc="V")
            self.emit("vector", lambda e, ii=i: e.scalar_tensor_tensor(self.tq[:], xt(ii), negm(ii),
                                                                       cb_rep, ALU.add, ALU.mult),
                      waits=[("V", max(v_sb, v_cb))], inc="V")
            self.emit("vector", lambda e, ii=i: e.scalar_tensor_tensor(self.tm[:], self.tq[:], rstd(ii),
                                                                       sb_rep, ALU.mult, ALU.add), inc="V")
            self.emit("vector", lambda e: e.reduce_max(asc, self.tm[:], axis=AX,
                                                       apply_absolute_value=True), inc="V")
            v_asc = self.semv["V"]
            self.emit("scalar", lambda e: e.activation(self.smalls[:, 16:17], asc, AF.Copy, bias=EPS_FFN),
                      waits=[("V", v_asc), ("S", self.s_wsbc)], inc="S")
            self.emit("scalar", lambda e, ii=i: e.activation(ffns(ii), self.smalls[:, 16:17], AF.Copy,
                                                             scale=self.smalls[:, 8:9]), inc="S")
            s_asc = self.semv["S"]
            self.emit("vector", lambda e: e.reciprocal(self.smalls[:, 4:5], self.smalls[:, 16:17]),
                      waits=[("S", s_asc)], inc="V")
            v_rec = self.semv["V"]
            self.emit("scalar", lambda e: e.activation(self.smalls[:, 20:21], self.smalls[:, 4:5],
                                                       AF.Copy, scale=127.0),
                      waits=[("V", v_rec)], inc="S")
            s_sinv = self.semv["S"]
            ztile = self.h1b[:, D * slot:D * (slot + 1)]
            w0 = [(f"TRZ{slot}", 16 * trz[slot])] if i >= 2 else []
            w0.append(("S", s_sinv))
            vh2 = self.semv["V"]
            self.emit("scalar", lambda e: e.activation(self.tq[:], self.tm[:], AF.Copy,
                                                       scale=self.smalls[:, 20:21], bias=128.0),
                      waits=[("V", vh2)] + w0, inc="S")
            sz = self.semv["S"]
            self.emit("vector", lambda e: e.tensor_copy(self.zi[:], self.tq[:]),
                      waits=[("S", sz)], inc="V")
            self.emit("vector", lambda e: e.scalar_tensor_tensor(self.tm[:], self.tq[:], -0.5, self.zi[:],
                                                                 ALU.add, ALU.is_ge), inc="V")
            self.emit("vector", lambda e, z_=ztile: e.scalar_tensor_tensor(z_, self.zi[:], -128.0, self.tm[:],
                                                                           ALU.add, ALU.add), inc="V")
            v_z = self.semv["V"]
            self.dma("scalar", zT_view[:, :, 128 * i:128 * (i + 1)], ztile,
                     waits=[("V", v_z)], inc=f"TRZ{slot}", transpose=True)
            trz[slot] += 1
            zinfo[i] = (slot, trz[slot])

        v_gb, _ = self.replicate_mod(5, 0, clamp=True)
        gb_rep = self.rep[:, 0:1024]
        for ci in range(NCH):
            self.emit("vector", lambda e, cc=ci: e.tensor_mul(
                self.bigA[:, 1024 * cc:1024 * (cc + 1)], self.bigA[:, 1024 * cc:1024 * (cc + 1)],
                gb_rep),
                waits=[("V", v_gb), ("TRW", 16 * self.trw_total)] if ci == 0 else [], inc="V")
        v_gbfold = self.semv["V"]

        evac_done = dict(self.psm_last)
        for i in range(NT):
            half = i % 2
            slot, k = zinfo[i]
            for ci in range(NCH):
                for p in range(2):
                    w0 = []
                    if ci == 0 and p == 0:
                        w0 = [(f"TRZ{slot}", 16 * k), ("TRW", 16 * self.trw_total),
                              ("V", max(v_gbfold, evac_done[half]))]
                    self.mm(self.psM[:, 1024 * half + 512 * p:1024 * half + 512 * (p + 1)],
                            self.bigB[:, T * ci + 128 * i:T * ci + 128 * (i + 1)],
                            self.bigA[:, 1024 * ci + 512 * p:1024 * ci + 512 * (p + 1)],
                            start=(ci == 0), stop=(ci == NCH - 1), waits=w0)
            t_f = self.semv["T"]
            self.emit("vector",
                      lambda e, ii=i, hh=half: e.scalar_tensor_tensor(
                          xt(ii), self.psM[:, 1024 * hh:1024 * (hh + 1)],
                          ffns(ii), xt(ii), ALU.mult, ALU.add),
                      waits=[("T", t_f)], inc="V")
            evac_done[half] = self.semv["V"]
            v_out = self.semv["V"]
            self.dma("sync", self.h_out[128 * i:128 * (i + 1), :], xt(i),
                     waits=[("V", v_out)], inc="OUT")

    def finish(self):
        if self.stop is None:
            self.emit("sync", lambda e: e.nop(), waits=[("OUT", 16 * NT)])
        else:
            allw = [(s_, v) for s_, v in self.semv.items() if v > 0 and s_ != "OUT"]
            self.emit("sync", lambda e: e.nop(), waits=allw)
            for i in range(NT):
                self.dma("sync", self.h_out[128 * i:128 * (i + 1), :],
                         self.x_sb[:, D * i:D * (i + 1)], inc="OUT")
        if self.debug:
            for name, src in [("mod", self.mod_row), ("bigA", self.bigA),
                              ("bigB", self.bigB), ("ckvT", self.ckvT),
                              ("aux1", self.aux1), ("wmodb", self.wmodb),
                              ("stats", self.stats), ("smalls", self.smalls)]:
                self.dma("sync", self.h_dbg[name][:, :], src[:, :],
                         waits=[("OUT", self.semv["OUT"])], inc="OUT")
        self.emit("sync", lambda e: e.nop(), waits=[("OUT", self.semv["OUT"])])

    def finalize(self):
        nc = self.nc
        with nc.Block() as block:
            @block.sync
            def _(e):
                for f in self.ops["sync"]:
                    f(e)

            @block.gpsimd
            def _(e):
                for f in self.ops["gpsimd"]:
                    f(e)

            @block.tensor
            def _(e):
                for f in self.ops["tensor"]:
                    f(e)

            @block.vector
            def _(e):
                for f in self.ops["vector"]:
                    f(e)

            @block.scalar
            def _(e):
                for f in self.ops["scalar"]:
                    f(e)
        return nc


def make_consts():
    inv = 1.0 / 10000.0 ** (np.arange(0, DR, 2, dtype=np.float64) / DR)
    t = np.arange(T, dtype=np.float64)
    fr = np.outer(t, inv)
    cossin = np.concatenate([np.cos(fr).T, np.sin(fr).T], axis=0).astype(NPBF)
    tri = np.triu(np.ones((128, 128), np.float32)).astype(NPBF)
    pairsel = np.zeros((16, 1024), np.float32)
    for c in range(8):
        pairsel[2 * c, 128 * c:128 * c + 64] = 1.0
        pairsel[2 * c + 1, 128 * c + 64:128 * c + 128] = 1.0
    return cossin, tri, pairsel.astype(NPBF)


_CACHE = {}


def kernel(L0_attn_x, L0_attn_W_kv_down, L0_attn_W_k_up, L0_attn_W_v_up, L0_attn_W_q,
           L0_attn_W_q_rope, L0_attn_W_k_rope, L0_attn_W_o, L0_ffn_weight, L0_c, L0_wmod):
    if "nc" not in _CACHE:
        _CACHE["nc"] = Builder(debug=False).finalize()
    nc = _CACHE["nc"]
    cossin, tri, pairsel = make_consts()
    x = np.ascontiguousarray(np.asarray(L0_attn_x, dtype=np.float32))
    c = np.asarray(L0_c, dtype=np.float32)
    common = {
        "w_kv_down": np.ascontiguousarray(L0_attn_W_kv_down, dtype=np.float32),
        "w_k_up": np.ascontiguousarray(L0_attn_W_k_up, dtype=np.float32),
        "w_v_up": np.ascontiguousarray(L0_attn_W_v_up, dtype=np.float32),
        "w_q": np.ascontiguousarray(L0_attn_W_q, dtype=np.float32),
        "w_q_rope": np.ascontiguousarray(L0_attn_W_q_rope, dtype=np.float32),
        "w_k_rope": np.ascontiguousarray(L0_attn_W_k_rope, dtype=np.float32),
        "w_o": np.ascontiguousarray(L0_attn_W_o, dtype=np.float32),
        "w_ffn": np.ascontiguousarray(L0_ffn_weight, dtype=np.float32),
        "w_mod": np.ascontiguousarray(L0_wmod, dtype=np.float32),
        "cossin": cossin,
        "trimask": tri,
        "pairsel": pairsel,
    }
    in_maps = []
    for bidx in range(B):
        m = dict(common)
        m["x"] = np.ascontiguousarray(x[bidx])
        m["cT"] = np.ascontiguousarray(c[bidx].reshape(8, 128).T)
        in_maps.append(m)
    res = run_bass_kernel_spmd(nc, in_maps, core_ids=list(range(B)))
    return np.stack([r["out"] for r in res.results], axis=0)


if __name__ == "__main__":
    b = Builder(debug=False)
    nc = b.finalize()
    print("build ok; instructions:", {k: len(v) for k, v in b.ops.items()})
